# version 08
# baseline (speedup 1.0000x reference)
import numpy as np
from contextlib import ExitStack

import concourse.bass as bass
import concourse.mybir as mybir
from concourse.bass_utils import run_bass_kernel_spmd

F32 = mybir.dt.float32
BF16 = mybir.dt.bfloat16
AF = mybir.ActivationFunctionType
ALU = mybir.AluOpType
AX = mybir.AxisListType

NCORES = 8
D = 1024
NCH = 8
NH = 16
NCHUNK = 8
TPC = 4096
NTAU = 8
NEXP = 20
ENGINES = ("sync", "gpsimd", "vector", "scalar", "tensor")


class Sem:
    __slots__ = ("name", "n", "h")

    def __init__(self, name):
        self.name, self.n, self.h = name, 0, None


class Prog:
    def __init__(self):
        self.ev = {e: [] for e in ENGINES}

    def op(self, eng, method, *args, incs=(), **kwargs):
        incs = list(incs)
        self.ev[eng].append(("op", method, args, kwargs, incs))
        for s, k in incs:
            s.n += k

    def wait(self, eng, sem, val):
        if val > 0:
            self.ev[eng].append(("wait", sem, val))

    def simulate(self):
        ptr = {e: 0 for e in ENGINES}
        vals = {}
        progress = True
        while progress:
            progress = False
            for e in ENGINES:
                while ptr[e] < len(self.ev[e]):
                    ev = self.ev[e][ptr[e]]
                    if ev[0] == "wait":
                        if vals.get(id(ev[1]), 0) < ev[2]:
                            break
                    else:
                        for s, k in ev[4]:
                            vals[id(s)] = vals.get(id(s), 0) + k
                    ptr[e] += 1
                    progress = True
        stuck = [e for e in ENGINES if ptr[e] < len(self.ev[e])]
        if stuck:
            lines = []
            for e in stuck:
                ev = self.ev[e][ptr[e]]
                lines.append(f"  {e}@{ptr[e]}/{len(self.ev[e])}: wait {ev[1].name} >= {ev[2]}"
                             f" (have {vals.get(id(ev[1]), 0)}, final {ev[1].n})")
            raise RuntimeError("SCHEDULE DEADLOCK:\n" + "\n".join(lines))

    def replay_into(self, block):
        def mk(eng_name):
            def run(E):
                for i, ev in enumerate(self.ev[eng_name]):
                    if ev[0] == "wait":
                        E.wait_ge(ev[1].h, ev[2])
                    else:
                        _, method, args, kwargs, incs = ev
                        try:
                            instr = getattr(E, method)(*args, **kwargs)
                        except Exception as exc:
                            raise RuntimeError(
                                f"replay fail {eng_name}[{i}] {method} "
                                f"args={[getattr(a, 'shape', a) for a in args]}") from exc
                        for s, k in incs:
                            instr = instr.then_inc(s.h, k)
            return run
        block.sync(mk("sync"))
        block.gpsimd(mk("gpsimd"))
        block.vector(mk("vector"))
        block.scalar(mk("scalar"))
        block.tensor(mk("tensor"))

    def n_instr(self):
        return {e: len(self.ev[e]) for e in ENGINES}

    @staticmethod
    def _ap_tensors(args):
        outs, ins = set(), set()
        first = True
        for a in args:
            t = getattr(a, "tensor", None)
            if t is not None:
                (outs if first else ins).add(t.name)
                first = False
        return outs, ins

    def add_engine_drains(self, eng):
        out = []
        reads, writes = set(), set()
        for ev in self.ev[eng]:
            if ev[0] != "op":
                out.append(ev)
                continue
            _, method, args, kwargs, incs = ev
            if method == "dma_start":
                out.append(ev)
                continue
            if method == "drain":
                out.append(ev)
                reads, writes = set(), set()
                continue
            o, i = self._ap_tensors(args)
            if (o & (reads | writes)) or (i & writes):
                out.append(("op", "drain", (), {}, []))
                reads, writes = set(), set()
            if incs:
                out.append(("op", method, args, kwargs, []))
                out.append(("op", "drain", (), {}, incs))
                reads, writes = set(), set()
            else:
                out.append(ev)
                writes |= o
                reads |= i
        self.ev[eng] = out


def build_nc():
    nc = bass.Bass("TRN2", target_bir_lowering=False)

    x_h = nc.dram_tensor("x", [D, TPC], F32, kind="ExternalInput").ap()
    wqkv_h = nc.dram_tensor("wqkv", [D, 3 * D], F32, kind="ExternalInput").ap()
    wout_h = nc.dram_tensor("wout", [D, D], F32, kind="ExternalInput").ap()
    gate_h = nc.dram_tensor("gate", [D, 16], F32, kind="ExternalInput").ap()
    bias_h = nc.dram_tensor("biase", [16], F32, kind="ExternalInput").ap()
    upT_h = nc.dram_tensor("upT", [NEXP, D, 512], F32, kind="ExternalInput").ap()
    downT_h = nc.dram_tensor("downT", [NEXP, 512, D], F32, kind="ExternalInput").ap()
    y_h = nc.dram_tensor("y", [D, TPC], F32, kind="ExternalOutput").ap()
    xmid_h = nc.dram_tensor("xmid", [D, TPC], BF16, kind="Internal").ap()
    srow_h = [nc.dram_tensor(f"srow{i}", [8192], BF16, kind="Internal").ap() for i in range(2)]
    rrow_h = [nc.dram_tensor(f"rrow{i}", [8192], BF16, kind="Internal").ap() for i in range(2)]

    p = Prog()

    with ExitStack() as octx:
        def osb(name, shape, dt):
            return octx.enter_context(nc.sbuf_tensor(name, shape, dt))

        def mksem(name):
            s = Sem(name)
            s.h = octx.enter_context(nc.semaphore(name))
            return s

        sT, sV, sS, sG = mksem("sT"), mksem("sV"), mksem("sS"), mksem("sG")
        sPh = mksem("sPh")
        dInit = mksem("dInit")
        dXin = [mksem(f"dXin{i}") for i in range(2)]
        dXmW = mksem("dXmW")
        dXmR = mksem("dXmR")
        dSrA = [mksem(f"dSrA{i}") for i in range(2)]
        dSrB = [mksem(f"dSrB{i}") for i in range(2)]
        dOut = [mksem(f"dOut{i}") for i in range(2)]
        dWgt = [mksem(f"dWgt{i}") for i in range(3)]

        iota1 = osb("iota1", [128, 128], F32)
        iota2 = osb("iota2", [128, 128], F32)
        identf32 = osb("identf32", [128, 128], F32)
        tri01 = osb("tri01", [128, 128], BF16)
        hselt = osb("hselt", [128, 128], BF16)
        E_sb = osb("E_sb", [16, 2048], BF16)
        ones128 = osb("ones128", [128, 1], BF16)
        onesrow = osb("onesrow", [1, 128], BF16)
        gate_sb = osb("gate_sb", [128, 128], BF16)
        bias_row = osb("bias_row", [1, 16], BF16)
        bias_tok = osb("bias_tok", [128, 16], F32)
        m_row = osb("m_row", [1, 512], F32)
        v_row = osb("v_row", [1, 512], F32)
        t_row = osb("t_row", [1, 512], F32)
        r_bf = osb("r_bf", [1, 512], BF16)
        mr_bf = osb("mr_bf", [1, 512], BF16)
        Rsb = osb("Rsb", [128, 512], BF16)
        MRsb = osb("MRsb", [128, 512], BF16)

        P = [octx.enter_context(nc.psum_tensor(f"P{i}", [128, 512], F32)) for i in range(8)]

        CONSTS = [1e-5, -0.5, 1e-30, 0.5, -0.125, -0.0625, 0.125, -1.0, 2.5, 1e-20]
        cpool = osb("cpool", [128, len(CONSTS)], F32)
        for i, cv in enumerate(CONSTS):
            nc.const_aps.register(F32, cv, cpool[:, i:i + 1])

        p.op("gpsimd", "iota", iota1[:], [[1, 128]], channel_multiplier=-1,
             allow_small_or_imprecise_dtypes=True)
        p.op("gpsimd", "iota", iota2[:], [[-128, 8], [64, 16]], channel_multiplier=-1,
             allow_small_or_imprecise_dtypes=True)
        p.op("gpsimd", "iota", E_sb[:], [[1, 16], [0, 128]], channel_multiplier=-1,
             allow_small_or_imprecise_dtypes=True)
        p.op("gpsimd", "memset", ones128[:], 1.0)
        for i, cv in enumerate(CONSTS):
            p.op("gpsimd", "memset", cpool[:, i:i + 1], cv)
        p.op("gpsimd", "memset", onesrow[:], 1.0, incs=[(sG, 1)])
        g_consts = sG.n
        p.op("gpsimd", "dma_start", gate_sb[:].rearrange("p (c e) -> p c e", e=16),
             gate_h.rearrange("(c p) e -> p c e", p=128), incs=[(dInit, 16)])
        p.op("gpsimd", "dma_start", bias_row[:], bias_h[None, :], incs=[(dInit, 16)])

        p.wait("vector", sG, g_consts)
        p.op("vector", "tensor_single_scalar", tri01[:], iota1[:], 0.0, ALU.is_ge)
        p.op("vector", "tensor_single_scalar", identf32[:], iota1[:], 0.0, ALU.is_equal)
        p.op("vector", "tensor_single_scalar", iota1[:], iota2[:], -64.0, ALU.is_gt)
        p.op("vector", "tensor_single_scalar", iota2[:], iota2[:], 0.0, ALU.is_le)
        p.op("vector", "tensor_single_scalar", E_sb[:], E_sb[:], 0.0, ALU.is_equal)
        p.op("vector", "tensor_mul", hselt[:], iota1[:], iota2[:], incs=[(sV, 1)])
        v_consts = sV.n

        p.wait("tensor", dInit, 64)
        p.wait("tensor", sV, v_consts)
        p.op("tensor", "matmul", P[4][:, 0:16], onesrow[:], bias_row[:],
             start=True, stop=True, incs=[(sT, 1)])
        p.wait("scalar", sT, sT.n)
        p.op("scalar", "copy", bias_tok[:], P[4][:, 0:16], incs=[(sS, 1)])

        sq2_free = [0, 0]
        ln_evac_s = 0
        ln_rows_v = 0

        with ExitStack() as ctx1:
            def sb1(name, shape, dt):
                return ctx1.enter_context(nc.sbuf_tensor(name, shape, dt))

            xin = [sb1(f"xin{i}", [128, NCH * 512], BF16) for i in range(2)]
            xn = sb1("xn", [128, NCH * 512], BF16)
            sq2 = [sb1(f"sq2_{i}", [128, 512], BF16) for i in range(2)]
            qf = sb1("qf", [128, NCH * 512], BF16)
            kroll = [sb1(f"kroll{i}", [128, NCH * 512], BF16) for i in range(2)]
            vroll = [sb1(f"vroll{i}", [128, 4 * 1024], BF16) for i in range(2)]
            probs = [sb1(f"probs{i}", [128, 512], BF16) for i in range(2)]
            aof = sb1("aof", [128, NCH * 512], BF16)
            sums_row = sb1("sums_row", [1, 8192], BF16)
            rrow = sb1("rrow", [1, 8192], BF16)
            spread = sb1("spread", [128, 64], BF16)
            rbsb = [sb1(f"rbsb{i}", [128, 512], BF16) for i in range(2)]
            qmax_sb = sb1("qmax_sb", [16, 1], F32)
            kmax_sb = sb1("kmax_sb", [16, 2], F32)
            u_sb = sb1("u_sb", [16, 1], F32)
            u2_sb = sb1("u2_sb", [16, 1], F32)
            B_sb = sb1("B_sb", [16, 1], F32)
            brow_bf = sb1("brow_bf", [1, 16], BF16)
            bias_all = sb1("bias_all", [128, 16], F32)
            wqkv_sb = sb1("wqkv_sb", [128, NCH * 3 * D], BF16)
            wout_sb = sb1("wout_sb", [128, NCH * D], BF16)

            p.op("gpsimd", "dma_start", wqkv_sb[:].rearrange("p (c n) -> p c n", n=3 * D),
                 wqkv_h.rearrange("(c p) n -> p c n", p=128), incs=[(dInit, 16)])
            p.op("gpsimd", "dma_start", wout_sb[:].rearrange("p (c n) -> p c n", n=D),
                 wout_h.rearrange("(c p) n -> p c n", p=128), incs=[(dInit, 16)])
            p.op("gpsimd", "dma_start", xin[0][:].rearrange("p (c t) -> p c t", t=512),
                 x_h.rearrange("(c p) t -> p c t", p=128)[:, :, 0:512], incs=[(dXin[0], 16)])
            nxin = [1, 0]

            wq = lambda fc, n: 3072 * fc + n

            pr_evac, pr_sem, pr_i = {}, {}, 0
            exp_done, pa_done, ridx = {}, {}, 0
            rb_evac, rb_i = {}, 0
            rb_last_s = 0
            sums_evac_val = 0
            qkv_done_prev = 0
            xadd_done_prev = 0
            xmid_stores = 0

            p.wait("tensor", dInit, 64)

            for g in range(NCHUNK):
                c = g % 4
                par = g % 2

                if g + 1 < NCHUNK:
                    np_ = (g + 1) % 2
                    p.wait("gpsimd", sV, xadd_done_prev)
                    p.op("gpsimd", "dma_start",
                         xin[np_][:].rearrange("p (c t) -> p c t", t=512),
                         x_h.rearrange("(c p) t -> p c t", p=128)[:, :, 512 * (g + 1):512 * (g + 2)],
                         incs=[(dXin[np_], 16)])
                    nxin[np_] += 1

                p.wait("vector", dXin[par], 16 * nxin[par])
                p.wait("tensor", dXin[par], 16 * nxin[par])
                if g > 0:
                    p.wait("tensor", sS, sums_evac_val)
                    p.wait("tensor", sS, rb_last_s)
                for fc in range(NCH):
                    p.op("tensor", "matmul", P[3][0:1, :], ones128[:],
                         xin[par][:, 512 * fc:512 * (fc + 1)],
                         start=(fc == 0), stop=(fc == NCH - 1))
                for fc in range(NCH):
                    p.wait("vector", sT, sq2_free[fc % 2])
                    p.op("vector", "tensor_mul", sq2[fc % 2][:],
                         xin[par][:, 512 * fc:512 * (fc + 1)],
                         xin[par][:, 512 * fc:512 * (fc + 1)], incs=[(sV, 1)])
                    p.wait("tensor", sV, sV.n)
                    p.op("tensor", "matmul", P[4][0:1, :], ones128[:], sq2[fc % 2][:],
                         start=(fc == 0), stop=(fc == NCH - 1), incs=[(sT, 1)])
                    sq2_free[fc % 2] = sT.n
                p.wait("vector", sT, sT.n)
                p.op("vector", "tensor_single_scalar", m_row[:], P[3][0:1, :], 1.0 / D, ALU.mult)
                p.op("vector", "tensor_single_scalar", v_row[:], P[4][0:1, :], 1.0 / D, ALU.mult)
                p.op("vector", "tensor_mul", t_row[:], m_row[:], m_row[:])
                p.op("vector", "tensor_sub", v_row[:], v_row[:], t_row[:], incs=[(sV, 1)])
                ln_rows_v = sV.n
                p.wait("scalar", sV, sV.n)
                p.op("scalar", "activation", t_row[:], v_row[:], AF.Ln, bias=1e-5)
                p.op("scalar", "activation", r_bf[:], t_row[:], AF.Exp, scale=-0.5, incs=[(sS, 1)])
                p.wait("vector", sS, sS.n)
                p.op("vector", "tensor_mul", mr_bf[:], m_row[:], r_bf[:], incs=[(sV, 1)])
                p.wait("tensor", sV, sV.n)
                p.op("tensor", "matmul", P[4][:], onesrow[:], r_bf[:], start=True, stop=True)
                p.op("tensor", "matmul", P[5][:], onesrow[:], mr_bf[:], start=True, stop=True,
                     incs=[(sT, 1)])
                p.wait("scalar", sT, sT.n)
                p.op("scalar", "copy", Rsb[:], P[4][:])
                p.op("scalar", "copy", MRsb[:], P[5][:], incs=[(sS, 1)])
                ln_evac_s = sS.n
                p.wait("vector", sS, ln_evac_s)
                p.wait("vector", dXmW, 16 * xmid_stores)
                p.wait("vector", sT, qkv_done_prev)
                xn_v = {}
                for fc in range(NCH):
                    p.wait("vector", sT, sq2_free[fc % 2])
                    p.op("vector", "tensor_mul", sq2[fc % 2][:],
                         xin[par][:, 512 * fc:512 * (fc + 1)], Rsb[:])
                    p.op("vector", "tensor_sub", xn[:, 512 * fc:512 * (fc + 1)],
                         sq2[fc % 2][:], MRsb[:], incs=[(sV, 1)])
                    xn_v[fc] = sV.n
                    sq2_free[fc % 2] = 0

                def mmgroup(dst_ap, oper_fn):
                    nonlocal pr_i
                    bank = 6 + (pr_i % 2)
                    if pr_i >= 2:
                        p.wait("tensor", pr_sem[pr_i - 2], pr_evac[pr_i - 2])
                    for fc in range(NCH):
                        p.wait("tensor", sV, xn_v[fc])
                        lhs, rhs = oper_fn(fc)
                        p.op("tensor", "matmul", P[bank][:], lhs, rhs,
                             start=(fc == 0), stop=(fc == NCH - 1),
                             incs=([(sT, 1)] if fc == NCH - 1 else []))
                    p.wait("scalar", sT, sT.n)
                    p.op("scalar", "copy", dst_ap, P[bank][:], incs=[(sS, 1)])
                    pr_evac[pr_i], pr_sem[pr_i] = sS.n, sS
                    pr_i += 1

                for m in range(NCH):
                    mmgroup(kroll[par][:, 512 * m:512 * (m + 1)],
                            lambda fc, m=m: (wqkv_sb[:, wq(fc, D + 128 * m):wq(fc, D + 128 * m) + 128],
                                             xn[:, 512 * fc:512 * (fc + 1)]))
                for tt in range(4):
                    for half in range(2):
                        mmgroup(vroll[par][:, 1024 * tt + 512 * half:1024 * tt + 512 * (half + 1)],
                                lambda fc, tt=tt, half=half: (
                                    xn[:, 512 * fc + 128 * tt:512 * fc + 128 * (tt + 1)],
                                    wqkv_sb[:, wq(fc, 2 * D + 512 * half):wq(fc, 2 * D + 512 * (half + 1))]))
                for m in range(NCH):
                    mmgroup(qf[:, 512 * m:512 * (m + 1)],
                            lambda fc, m=m: (wqkv_sb[:, wq(fc, 128 * m):wq(fc, 128 * m) + 128],
                                             xn[:, 512 * fc:512 * (fc + 1)]))
                qkv_done_prev = sT.n

                p.wait("tensor", sS, pr_evac[pr_i - 1])
                p.wait("vector", sS, pr_evac[pr_i - 1])
                for which, bank in (("q", 3), ("k", 5)):
                    src = qf if which == "q" else kroll[par]
                    for fc in range(NCH):
                        p.wait("vector", sT, sq2_free[fc % 2])
                        p.op("vector", "tensor_mul", sq2[fc % 2][:],
                             src[:, 512 * fc:512 * (fc + 1)], src[:, 512 * fc:512 * (fc + 1)],
                             incs=[(sV, 1)])
                        p.wait("tensor", sV, sV.n)
                        p.op("tensor", "matmul", P[bank][0:16, :],
                             hselt[:, 16 * fc:16 * (fc + 1)], sq2[fc % 2][:],
                             start=(fc == 0), stop=(fc == NCH - 1), incs=[(sT, 1)])
                        sq2_free[fc % 2] = sT.n
                p.wait("vector", sT, sT.n)
                p.op("vector", "reduce_max", qmax_sb[:], P[3][0:16, :], AX.X)
                if c > 0:
                    p.op("vector", "reduce_max", kmax_sb[:, par:par + 1], P[5][0:16, :], AX.X)
                    p.op("vector", "tensor_max", u_sb[:], kmax_sb[:, 0:1], kmax_sb[:, 1:2],
                         incs=[(sV, 1)])
                    p.wait("scalar", sV, sV.n)
                    p.op("scalar", "add", u2_sb[:], u_sb[:], qmax_sb[:], incs=[(sS, 1)])
                else:
                    p.op("vector", "reduce_max", kmax_sb[:, par:par + 1], P[5][0:16, :], AX.X,
                         incs=[(sV, 1)])
                    p.wait("scalar", sV, sV.n)
                    p.op("scalar", "add", u2_sb[:], kmax_sb[:, par:par + 1], qmax_sb[:],
                         incs=[(sS, 1)])
                p.wait("tensor", sS, sS.n)
                p.op("tensor", "transpose", P[4][0:1, 0:16], u2_sb[:], identf32[0:16, 0:16],
                     incs=[(sT, 1)])
                p.wait("scalar", sT, sT.n)
                p.op("scalar", "copy", brow_bf[:], P[4][0:1, 0:16], incs=[(sS, 1)])
                p.wait("tensor", sS, sS.n)
                p.op("tensor", "matmul", P[4][:, 0:16], onesrow[:], brow_bf[:],
                     start=True, stop=True, incs=[(sT, 1)])
                p.wait("scalar", sT, sT.n)
                p.op("scalar", "activation", bias_all[:], P[4][:, 0:16], AF.Copy,
                     scale=-0.0625, incs=[(sS, 1)])

                for h in range(NH):
                    fch, off = h // 2, 64 * (h % 2)
                    kts = ([("prev", kt) for kt in range(4)] if c > 0 else []) \
                        + [("cur", kt) for kt in range(4)]
                    nkts = len(kts)
                    for idx, (which, kt) in enumerate(kts):
                        rb = ridx % 2
                        qlo = 0 if which == "prev" else 128 * kt
                        ksrc = kroll[(g - 1) % 2] if which == "prev" else kroll[par]
                        vsrc = vroll[(g - 1) % 2] if which == "prev" else vroll[par]
                        if ridx >= 2:
                            p.wait("tensor", sS, exp_done[ridx - 2])
                        p.op("tensor", "matmul", P[rb][:, qlo:512],
                             ksrc[off:off + 64, 512 * fch + 128 * kt:512 * fch + 128 * (kt + 1)],
                             qf[off:off + 64, 512 * fch + qlo:512 * (fch + 1)],
                             start=True, stop=True, incs=[(sT, 1)])
                        p.wait("scalar", sT, sT.n)
                        if ridx >= 2:
                            p.wait("scalar", sT, pa_done[ridx - 2])
                        p.op("scalar", "activation", probs[rb][:, qlo:512], P[rb][:, qlo:512],
                             AF.Exp, bias=bias_all[:, h:h + 1], scale=0.125, incs=[(sS, 1)])
                        exp_done[ridx] = sS.n
                        if which == "cur":
                            p.wait("vector", sS, exp_done[ridx])
                            p.op("vector", "tensor_mul", probs[rb][:, qlo:qlo + 128],
                                 probs[rb][:, qlo:qlo + 128], tri01[:], incs=[(sV, 1)])
                            p.wait("tensor", sV, sV.n)
                        p.wait("tensor", sS, exp_done[ridx])
                        last = (idx == nkts - 1)
                        p.op("tensor", "matmul", P[3][0:1, qlo:512], ones128[:],
                             probs[rb][:, qlo:512], start=(idx == 0), stop=last)
                        p.op("tensor", "matmul", P[2][off:off + 64, qlo:512],
                             vsrc[:, 1024 * kt + 64 * h:1024 * kt + 64 * (h + 1)],
                             probs[rb][:, qlo:512], start=(idx == 0), stop=last,
                             incs=[(sT, 1)])
                        pa_done[ridx] = sT.n
                        ridx += 1
                    p.wait("scalar", sT, sT.n)
                    p.op("scalar", "copy", sums_row[0:1, 512 * h:512 * (h + 1)], P[3][0:1, :])
                    p.op("scalar", "copy", aof[off:off + 64, 512 * fch:512 * (fch + 1)],
                         P[2][off:off + 64, :], incs=[(sS, 1)])
                    sums_evac_val = sS.n
                    p.wait("tensor", sS, sums_evac_val)

                p.wait("sync", sS, sums_evac_val)
                nA = 32 * (g // 2)
                p.op("sync", "dma_start", srow_h[par][None, :], sums_row[:],
                     incs=[(dSrA[par], 16)])
                p.wait("sync", dSrA[par], nA + 16)
                p.op("sync", "dma_start", spread[:],
                     srow_h[par].rearrange("(p j) -> p j", p=128), incs=[(dSrA[par], 16)])
                p.wait("vector", dSrA[par], nA + 32)
                p.op("vector", "reciprocal", spread[:], spread[:], incs=[(sV, 1)])
                p.wait("sync", sV, sV.n)
                nB = 32 * (g // 2)
                p.op("sync", "dma_start", rrow_h[par].rearrange("(p j) -> p j", p=128),
                     spread[:], incs=[(dSrB[par], 16)])
                p.wait("sync", dSrB[par], nB + 16)
                p.op("sync", "dma_start", rrow[:], rrow_h[par][None, :], incs=[(dSrB[par], 16)])
                p.wait("tensor", dSrB[par], nB + 32)
                for h in range(NH):
                    fch, off = h // 2, 64 * (h % 2)
                    bank = 4 + (rb_i % 2)
                    if rb_i >= 2:
                        p.wait("tensor", sS, rb_evac[rb_i - 2])
                    p.op("tensor", "matmul", P[bank][off:off + 64, :], onesrow[0:1, 0:64],
                         rrow[0:1, 512 * h:512 * (h + 1)], start=True, stop=True, incs=[(sT, 1)])
                    p.wait("scalar", sT, sT.n)
                    p.op("scalar", "copy", rbsb[rb_i % 2][off:off + 64, :],
                         P[bank][off:off + 64, :], incs=[(sS, 1)])
                    rb_evac[rb_i] = sS.n
                    p.wait("vector", sS, rb_evac[rb_i])
                    p.op("vector", "tensor_mul", aof[off:off + 64, 512 * fch:512 * (fch + 1)],
                         aof[off:off + 64, 512 * fch:512 * (fch + 1)],
                         rbsb[rb_i % 2][off:off + 64, :], incs=[(sV, 1)])
                    rb_i += 1
                rb_last_s = rb_evac[rb_i - 1]
                v_aof = sV.n

                p.wait("tensor", sV, v_aof)
                for m in range(NCH):
                    bank = 6 + (pr_i % 2)
                    if pr_i >= 2:
                        p.wait("tensor", pr_sem[pr_i - 2], pr_evac[pr_i - 2])
                    for fc in range(NCH):
                        p.op("tensor", "matmul", P[bank][:],
                             wout_sb[:, 1024 * fc + 128 * m:1024 * fc + 128 * (m + 1)],
                             aof[:, 512 * fc:512 * (fc + 1)],
                             start=(fc == 0), stop=(fc == NCH - 1),
                             incs=([(sT, 1)] if fc == NCH - 1 else []))
                    p.wait("vector", sT, sT.n)
                    p.op("vector", "tensor_add", xn[:, 512 * m:512 * (m + 1)], P[bank][:],
                         xin[par][:, 512 * m:512 * (m + 1)], incs=[(sV, 1)])
                    pr_evac[pr_i], pr_sem[pr_i] = sV.n, sV
                    pr_i += 1
                xadd_done_prev = sV.n
                p.wait("sync", sV, xadd_done_prev)
                p.op("sync", "dma_start",
                     xmid_h.rearrange("(c p) t -> p c t", p=128)[:, :, 512 * g:512 * (g + 1)],
                     xn[:].rearrange("p (c t) -> p c t", t=512), incs=[(dXmW, 16)])
                xmid_stores += 1

            p.wait("sync", dXmW, 16 * xmid_stores)
            p.wait("gpsimd", dXin[0], 16 * nxin[0])
            p.wait("gpsimd", dXin[1], 16 * nxin[1])
            p.op("vector", "tensor_copy", t_row[:], m_row[:], incs=[(sPh, 1)])
            p.op("scalar", "copy", v_row[:], m_row[:], incs=[(sPh, 1)])
            for e in ("sync", "gpsimd", "tensor"):
                p.op(e, "nop", incs=[(sPh, 1)])
            for e in ENGINES:
                p.wait(e, sPh, 5)

        with ExitStack() as ctx2:
            def sb2(name, shape, dt):
                return ctx2.enter_context(nc.sbuf_tensor(name, shape, dt))

            xres = sb2("xres", [128, NCH * TPC], BF16)
            xn2 = sb2("xn2", [128, NCH * TPC], BF16)
            wb = [sb2(f"wb{i}", [128, 4096], BF16) for i in range(3)]
            H = [sb2(f"H{i}", [128, 512], BF16) for i in range(4)]
            h1 = [sb2(f"h1_{i}", [128, 512], BF16) for i in range(2)]
            cwb = [sb2(f"cwb{i}", [128, 512], BF16) for i in range(2)]
            sq2b = h1
            scorestok = sb2("scorestok", [128, 512], F32)
            sctok = sb2("sctok", [128, 512], F32)
            m32 = sb2("m32", [128, 32], F32)
            eqb = sb2("eqb", [128, 512], BF16)
            pfx = [sb2(f"pfx{i}", [128, 512], BF16) for i in range(2)]
            selb = sb2("selb", [128, 512], BF16)
            selsum = sb2("selsum", [128, 512], BF16)
            wtok = sb2("wtok", [128, 512], F32)
            exl = wtok
            wsum = sb2("wsum", [128, 32], F32)
            wsum2 = sb2("wsum2", [128, 32], F32)
            winv = sb2("winv", [128, 32], F32)
            gp6 = sb2("gp6", [128, 768], F32)
            gtree = sb2("gtree", [128, 512], F32)
            gcmp = gp6
            granks = gtree
            gsc = sb2("gsc", [128, 128], F32)
            gmask = sb2("gmask", [128, 128], F32)
            cmb_f = sb2("cmb_f", [16, 4096], BF16)
            stage = [sb2(f"stage{i}", [128, 512], F32) for i in range(2)]

            p.op("sync", "dma_start", xres[:].rearrange("p (c t) -> p c t", t=TPC),
                 xmid_h.rearrange("(c p) t -> p c t", p=128), incs=[(dXmR, 16)])
            p.wait("vector", dXmR, 16)
            p.wait("tensor", dXmR, 16)

            def load_w(slot, e, is_up):
                src = upT_h[e] if is_up else downT_h[e]
                inner = 512 if is_up else D
                p.op("gpsimd", "dma_start",
                     wb[slot][:].rearrange("p (c i) -> p c i", i=inner),
                     src.rearrange("(c p) i -> p c i", p=128), incs=[(dWgt[slot], 16)])

            slot_loads = [0, 0, 0]
            su = [0] * NEXP
            sd = [0] * NEXP
            su[0], sd[0], su[1] = 0, 1, 2
            for e in range(1, NEXP):
                if e >= 2:
                    su[e] = sd[e - 2]
                sd[e] = su[e - 1]
            load_w(0, 0, True); slot_loads[0] += 1
            load_w(1, 0, False); slot_loads[1] += 1
            load_w(2, 1, True); slot_loads[2] += 1

            logit_mm_done = 0
            sq2_free = [0, 0]
            for tau in range(NTAU):
                if tau > 0:
                    p.wait("tensor", sS, ln_evac_s)
                    p.wait("tensor", sV, ln_rows_v)
                for fc in range(NCH):
                    p.op("tensor", "matmul", P[3][0:1, :], ones128[:],
                         xres[:, TPC * fc + 512 * tau:TPC * fc + 512 * (tau + 1)],
                         start=(fc == 0), stop=(fc == NCH - 1))
                for fc in range(NCH):
                    p.wait("vector", sT, sq2_free[fc % 2])
                    p.op("vector", "tensor_mul", sq2b[fc % 2][:],
                         xres[:, TPC * fc + 512 * tau:TPC * fc + 512 * (tau + 1)],
                         xres[:, TPC * fc + 512 * tau:TPC * fc + 512 * (tau + 1)], incs=[(sV, 1)])
                    p.wait("tensor", sV, sV.n)
                    p.op("tensor", "matmul", P[4][0:1, :], ones128[:], sq2b[fc % 2][:],
                         start=(fc == 0), stop=(fc == NCH - 1), incs=[(sT, 1)])
                    sq2_free[fc % 2] = sT.n
                p.wait("vector", sT, sT.n)
                p.op("vector", "tensor_single_scalar", m_row[:], P[3][0:1, :], 1.0 / D, ALU.mult)
                p.op("vector", "tensor_single_scalar", v_row[:], P[4][0:1, :], 1.0 / D, ALU.mult)
                p.op("vector", "tensor_mul", t_row[:], m_row[:], m_row[:])
                p.op("vector", "tensor_sub", v_row[:], v_row[:], t_row[:], incs=[(sV, 1)])
                ln_rows_v = sV.n
                p.wait("scalar", sV, sV.n)
                p.op("scalar", "activation", t_row[:], v_row[:], AF.Ln, bias=1e-5)
                p.op("scalar", "activation", r_bf[:], t_row[:], AF.Exp, scale=-0.5, incs=[(sS, 1)])
                p.wait("vector", sS, sS.n)
                p.op("vector", "tensor_mul", mr_bf[:], m_row[:], r_bf[:], incs=[(sV, 1)])
                p.wait("tensor", sV, sV.n)
                p.op("tensor", "matmul", P[4][:], onesrow[:], r_bf[:], start=True, stop=True)
                p.op("tensor", "matmul", P[5][:], onesrow[:], mr_bf[:], start=True, stop=True,
                     incs=[(sT, 1)])
                p.wait("scalar", sT, sT.n)
                p.op("scalar", "copy", Rsb[:], P[4][:])
                p.op("scalar", "copy", MRsb[:], P[5][:], incs=[(sS, 1)])
                ln_evac_s = sS.n
                p.wait("vector", sS, ln_evac_s)
                for fc in range(NCH):
                    p.wait("vector", sT, sq2_free[fc % 2])
                    p.op("vector", "tensor_mul", sq2b[fc % 2][:],
                         xres[:, TPC * fc + 512 * tau:TPC * fc + 512 * (tau + 1)], Rsb[:])
                    p.op("vector", "tensor_sub",
                         xn2[:, TPC * fc + 512 * tau:TPC * fc + 512 * (tau + 1)],
                         sq2b[fc % 2][:], MRsb[:], incs=[(sV, 1)])
                    sq2_free[fc % 2] = 0
                xn2_tau_v = sV.n
                p.wait("tensor", sV, xn2_tau_v)
                for tt in range(4):
                    for fc in range(NCH):
                        p.op("tensor", "matmul",
                             P[6][:, 64 * tau + 16 * tt:64 * tau + 16 * (tt + 1)],
                             xn2[:, TPC * fc + 512 * tau + 128 * tt:TPC * fc + 512 * tau + 128 * (tt + 1)],
                             gate_sb[:, 16 * fc:16 * (fc + 1)],
                             start=(fc == 0), stop=(fc == NCH - 1),
                             incs=([(sT, 1)] if fc == NCH - 1 else []))
                logit_mm_done = sT.n

            p.wait("scalar", sT, logit_mm_done)
            for j in range(4):
                p.op("scalar", "activation", exl[:, 128 * j:128 * (j + 1)],
                     P[6][:, 128 * j:128 * (j + 1)], AF.Exp, scale=-1.0,
                     incs=([(sS, 1)] if j == 3 else []))
            s_exl = sS.n
            p.wait("vector", sS, s_exl)
            p.op("vector", "tensor_single_scalar", exl[:], exl[:], 1.0, ALU.add)
            p.op("vector", "reciprocal", scorestok[:], exl[:])
            sct3 = sctok[:].rearrange("p (t e) -> p t e", e=16)
            sco3 = scorestok[:].rearrange("p (t e) -> p t e", e=16)
            p.op("vector", "tensor_tensor", sct3, sco3,
                 bias_tok[:, None, :].broadcast_to([128, 32, 16]), ALU.add)

            scg = sctok[:].rearrange("p (gg e) -> p gg e", e=4)
            pairs = [(0, 1), (0, 2), (0, 3), (1, 2), (1, 3), (2, 3)]
            for i, (a, b) in enumerate(pairs):
                p.op("vector", "tensor_tensor", gp6[:, 128 * i:128 * (i + 1), None],
                     scg[:, :, a:a + 1], scg[:, :, b:b + 1], ALU.add)
            p.op("vector", "tensor_max", gtree[:, 0:128], gp6[:, 0:128], gp6[:, 128:256])
            p.op("vector", "tensor_max", gtree[:, 128:256], gp6[:, 256:384], gp6[:, 384:512])
            p.op("vector", "tensor_max", gtree[:, 256:384], gp6[:, 512:640], gp6[:, 640:768])
            p.op("vector", "tensor_max", gtree[:, 384:512], gtree[:, 0:128], gtree[:, 128:256])
            p.op("vector", "tensor_copy", pfx[0][:], scorestok[:])
            p.op("vector", "tensor_max", gsc[:], gtree[:, 384:512], gtree[:, 256:384])
            p.op("vector", "tensor_copy", pfx[1][:], scorestok[:])
            g3 = gsc[:].rearrange("p (t g) -> p t g", g=4)
            gm3 = gmask[:].rearrange("p (t g) -> p t g", g=4)
            ci = 0
            for j in range(4):
                for mth in range(4):
                    if mth == j:
                        continue
                    opc = ALU.is_ge if mth < j else ALU.is_gt
                    p.op("vector", "tensor_tensor", gcmp[:, 32 * ci:32 * (ci + 1), None],
                         g3[:, :, mth:mth + 1], g3[:, :, j:j + 1], opc)
                    ci += 1
            for j in range(4):
                p.op("vector", "tensor_add", granks[:, 32 * j:32 * (j + 1)],
                     gcmp[:, 96 * j:96 * j + 32], gcmp[:, 96 * j + 32:96 * j + 64])
            for j in range(4):
                p.op("vector", "tensor_add", granks[:, 32 * j:32 * (j + 1)],
                     granks[:, 32 * j:32 * (j + 1)], gcmp[:, 96 * j + 64:96 * j + 96])
            for j in range(4):
                p.op("vector", "tensor_single_scalar", gm3[:, :, j:j + 1],
                     granks[:, 32 * j:32 * (j + 1), None], 1.5, ALU.is_le)
            p.op("vector", "tensor_add", gtree[:, 128:256], gsc[:], gsc[:])
            p.op("vector", "tensor_add", gtree[:, 128:256], gsc[:], gsc[:])
            p.op("vector", "tensor_tensor", scg, scg,
                 gmask[:, :, None].broadcast_to([128, 128, 4]), ALU.mult)

            eq3 = eqb[:].rearrange("p (t e) -> p t e", e=16)
            for r in range(4):
                p.op("vector", "reduce_max", m32[:], sct3, AX.X)
                p.op("vector", "tensor_tensor", eq3, sct3,
                     m32[:, :, None].broadcast_to([128, 32, 16]), ALU.is_ge)
                cur = eqb
                for si, s in enumerate((1, 2, 4, 8)):
                    nxt = pfx[si % 2]
                    cv = cur[:].rearrange("p (t e) -> p t e", e=16)
                    nv = nxt[:].rearrange("p (t e) -> p t e", e=16)
                    p.op("vector", "tensor_copy", nv[:, :, 0:s], cv[:, :, 0:s])
                    p.op("vector", "tensor_tensor", nv[:, :, s:16], cv[:, :, s:16],
                         cv[:, :, 0:16 - s], ALU.add)
                    cur = nxt
                p.op("vector", "tensor_single_scalar", gp6[:, 0:512], cur[:], 1.0,
                     ALU.is_equal)
                p.op("vector", "tensor_mul", selb[:], eqb[:], gp6[:, 0:512])
                if r == 0:
                    p.op("vector", "tensor_copy", selsum[:], selb[:])
                else:
                    p.op("vector", "tensor_add", selsum[:], selsum[:], selb[:])
                if r < 3:
                    p.op("vector", "scalar_tensor_tensor", sctok[:], selb[:], -1e30,
                         sctok[:], ALU.mult, ALU.add)
            p.op("vector", "tensor_mul", wtok[:], scorestok[:], selsum[:])
            w3 = wtok[:].rearrange("p (t e) -> p t e", e=16)
            p.op("vector", "reduce_sum", wsum[:], w3, AX.X)
            p.op("vector", "tensor_scalar_add", wsum2[:], wsum[:],
                 cpool[:, CONSTS.index(1e-20):CONSTS.index(1e-20) + 1])
            p.op("vector", "tensor_add", gp6[:, 0:512], scorestok[:], scorestok[:])
            p.op("vector", "tensor_add", gp6[:, 0:512], scorestok[:], scorestok[:])
            p.op("vector", "reciprocal", winv[:], wsum2[:])
            p.op("vector", "tensor_add", gp6[:, 0:512], scorestok[:], scorestok[:])
            p.op("vector", "tensor_tensor", w3, w3,
                 winv[:, :, None].broadcast_to([128, 32, 16]), ALU.mult, incs=[(sV, 1)])
            v_comb = sV.n

            p.wait("tensor", sV, v_comb)
            p.wait("tensor", sS, ln_evac_s)
            cmb_evac = {}
            for tt in range(32):
                bank = 4 + (tt % 2)
                if tt >= 2:
                    p.wait("tensor", sS, cmb_evac[tt - 2])
                p.op("tensor", "transpose", P[bank][0:16, 0:128],
                     wtok[:, 16 * tt:16 * (tt + 1)], identf32[:], incs=[(sT, 1)])
                p.wait("scalar", sT, sT.n)
                p.op("scalar", "copy", cmb_f[:, 128 * tt:128 * (tt + 1)],
                     P[bank][0:16, 0:128], incs=[(sS, 1)])
                cmb_evac[tt] = sS.n
            s_cmb = sS.n

            H_free = 0
            upp_free = [0, 0]
            dwp_free = [0, 0]
            cwb_free = [0, 0]
            cwb_s = [0, 0]
            h1_free = [0, 0]
            stage_dma = [0, 0]

            for e in range(NEXP):
                p.wait("tensor", dWgt[su[e]], 16 * slot_loads[su[e]])
                first_dn_wait = 16 * slot_loads[sd[e]]
                for tau in range(NTAU):
                    if e < 16:
                        cb = tau % 2
                        p.wait("tensor", sV, cwb_free[cb])
                        p.wait("tensor", sS, s_cmb)
                        p.op("tensor", "matmul", P[6 + cb][:], E_sb[:, 128 * e:128 * (e + 1)],
                             cmb_f[:, 512 * tau:512 * (tau + 1)], start=True, stop=True,
                             incs=[(sT, 1)])
                        p.wait("scalar", sT, sT.n)
                        p.op("scalar", "activation", cwb[cb][:], P[6 + cb][:], AF.Copy,
                             scale=2.5, incs=[(sS, 1)])
                        cwb_s[cb] = sS.n
                    h_last_v = 0
                    for m in range(4):
                        ub = m % 2
                        p.wait("tensor", sS, upp_free[ub])
                        for kc in range(NCH):
                            p.op("tensor", "matmul", P[ub][:],
                                 wb[su[e]][:, 512 * kc + 128 * m:512 * kc + 128 * (m + 1)],
                                 xn2[:, TPC * kc + 512 * tau:TPC * kc + 512 * (tau + 1)],
                                 start=(kc == 0), stop=(kc == NCH - 1),
                                 incs=([(sT, 1)] if kc == NCH - 1 else []))
                        p.wait("scalar", sT, sT.n)
                        p.wait("scalar", sV, h1_free[ub])
                        p.op("scalar", "activation", h1[ub][:], P[ub][:], AF.Relu,
                             incs=[(sS, 1)])
                        upp_free[ub] = sS.n
                        p.wait("vector", sS, upp_free[ub])
                        p.wait("vector", sT, H_free)
                        if e < 16:
                            p.wait("vector", sS, cwb_s[tau % 2])
                            p.op("vector", "tensor_mul", H[m][:], h1[ub][:], cwb[tau % 2][:])
                            p.op("vector", "tensor_mul", H[m][:], H[m][:], h1[ub][:],
                                 incs=[(sV, 1)])
                        else:
                            p.op("vector", "tensor_mul", H[m][:], h1[ub][:], h1[ub][:],
                                 incs=[(sV, 1)])
                        h1_free[ub] = sV.n
                        h_last_v = sV.n
                    if e < 16:
                        cwb_free[tau % 2] = h_last_v
                    p.wait("tensor", sV, h_last_v)
                    p.wait("tensor", dWgt[sd[e]], first_dn_wait)
                    for n in range(NCH):
                        db = n % 2
                        p.wait("tensor", sV, dwp_free[db])
                        for kc in range(4):
                            p.op("tensor", "matmul", P[2 + db][:],
                                 wb[sd[e]][:, 1024 * kc + 128 * n:1024 * kc + 128 * (n + 1)],
                                 H[kc][:], start=(kc == 0), stop=(kc == 3),
                                 incs=([(sT, 1)] if kc == 3 else []))
                        p.wait("vector", sT, sT.n)
                        xslice = xres[:, TPC * n + 512 * tau:TPC * n + 512 * (tau + 1)]
                        if e < NEXP - 1:
                            p.op("vector", "tensor_add", xslice, P[2 + db][:], xslice,
                                 incs=[(sV, 1)])
                        else:
                            sp = n % 2
                            p.wait("vector", dOut[sp], 16 * stage_dma[sp])
                            p.op("vector", "tensor_add", stage[sp][:], P[2 + db][:], xslice,
                                 incs=[(sV, 1)])
                            p.wait("sync", sV, sV.n)
                            p.op("sync", "dma_start",
                                 y_h[128 * n:128 * (n + 1), 512 * tau:512 * (tau + 1)],
                                 stage[sp][:], incs=[(dOut[sp], 16)])
                            stage_dma[sp] += 1
                        dwp_free[db] = sV.n
                    H_free = sT.n
                if e + 1 < NEXP:
                    p.wait("gpsimd", sT, H_free)
                    load_w(su[e], e + 1, False)
                    slot_loads[su[e]] += 1
                if e + 2 < NEXP:
                    p.wait("gpsimd", sT, H_free)
                    load_w(sd[e], e + 2, True)
                    slot_loads[sd[e]] += 1

            p.wait("sync", dOut[0], 16 * stage_dma[0])
            p.wait("sync", dOut[1], 16 * stage_dma[1])

        for _e in ("vector", "scalar"):
            p.add_engine_drains(_e)
        p.simulate()
        with nc.allow_low_precision(reason="bf16 residual stream / exact small-int masks; "
                                           "error budget analyzed vs rel-L2 4.8e-2 bar"):
            with nc.Block() as block:
                p.replay_into(block)
    return nc, p


_NC_CACHE = {}


def _get_nc():
    if "nc" not in _NC_CACHE:
        _NC_CACHE["nc"] = build_nc()
    return _NC_CACHE["nc"]


def kernel(L0_attn_x, L0_attn_w_qkv, L0_attn_w_out, L0_ffn_gate_w, L0_ffn_bias_e,
           L0_ffn_up_w, L0_ffn_down_w, L0_ffn_shared_up, L0_ffn_shared_down):
    x = np.asarray(L0_attn_x, dtype=np.float32)
    wqkv = np.ascontiguousarray(np.asarray(L0_attn_w_qkv, dtype=np.float32))
    wout = np.ascontiguousarray(np.asarray(L0_attn_w_out, dtype=np.float32))
    gate = np.ascontiguousarray(np.asarray(L0_ffn_gate_w, dtype=np.float32).T)
    biase = np.ascontiguousarray(np.asarray(L0_ffn_bias_e, dtype=np.float32))
    up_w = np.asarray(L0_ffn_up_w, dtype=np.float32)
    down_w = np.asarray(L0_ffn_down_w, dtype=np.float32)
    sup = np.asarray(L0_ffn_shared_up, dtype=np.float32)
    sdn = np.asarray(L0_ffn_shared_down, dtype=np.float32)

    upT = np.ascontiguousarray(np.concatenate(
        [up_w.transpose(0, 2, 1), sup.reshape(4, 512, D).transpose(0, 2, 1)], axis=0))
    downT = np.ascontiguousarray(np.concatenate(
        [down_w.transpose(0, 2, 1), np.ascontiguousarray(sdn.T).reshape(4, 512, D)], axis=0))

    nc, _ = _get_nc()
    in_maps = []
    for ci in range(NCORES):
        xc = np.ascontiguousarray(x[2 * ci:2 * ci + 2].reshape(TPC, D).T)
        in_maps.append({"x": xc, "wqkv": wqkv, "wout": wout, "gate": gate,
                        "biase": biase, "upT": upT, "downT": downT})
    res = run_bass_kernel_spmd(nc, in_maps, core_ids=list(range(NCORES)))
    y = np.empty((16, 2048, 1024), dtype=np.float32)
    for ci in range(NCORES):
        y[2 * ci:2 * ci + 2] = res.results[ci]["y"].T.reshape(2, 2048, D)
    return y


if __name__ == "__main__":
    nc, p = build_nc()
    print("instructions:", p.n_instr())


# version 09
# speedup vs baseline: 1.0181x; 1.0181x over previous
import numpy as np
from contextlib import ExitStack

import concourse.bass as bass
import concourse.mybir as mybir
from concourse.bass_utils import run_bass_kernel_spmd

F32 = mybir.dt.float32
BF16 = mybir.dt.bfloat16
AF = mybir.ActivationFunctionType
ALU = mybir.AluOpType
AX = mybir.AxisListType

NCORES = 8
D = 1024
NCH = 8
NH = 16
NCHUNK = 8
TPC = 4096
NTAU = 8
NEXP = 20
ENGINES = ("sync", "gpsimd", "vector", "scalar", "tensor")


class Sem:
    __slots__ = ("name", "n", "h")

    def __init__(self, name):
        self.name, self.n, self.h = name, 0, None


class Prog:
    def __init__(self):
        self.ev = {e: [] for e in ENGINES}

    def op(self, eng, method, *args, incs=(), **kwargs):
        incs = list(incs)
        self.ev[eng].append(("op", method, args, kwargs, incs))
        for s, k in incs:
            s.n += k

    def wait(self, eng, sem, val):
        if val > 0:
            self.ev[eng].append(("wait", sem, val))

    def simulate(self):
        ptr = {e: 0 for e in ENGINES}
        vals = {}
        progress = True
        while progress:
            progress = False
            for e in ENGINES:
                while ptr[e] < len(self.ev[e]):
                    ev = self.ev[e][ptr[e]]
                    if ev[0] == "wait":
                        if vals.get(id(ev[1]), 0) < ev[2]:
                            break
                    else:
                        for s, k in ev[4]:
                            vals[id(s)] = vals.get(id(s), 0) + k
                    ptr[e] += 1
                    progress = True
        stuck = [e for e in ENGINES if ptr[e] < len(self.ev[e])]
        if stuck:
            lines = []
            for e in stuck:
                ev = self.ev[e][ptr[e]]
                lines.append(f"  {e}@{ptr[e]}/{len(self.ev[e])}: wait {ev[1].name} >= {ev[2]}"
                             f" (have {vals.get(id(ev[1]), 0)}, final {ev[1].n})")
            raise RuntimeError("SCHEDULE DEADLOCK:\n" + "\n".join(lines))

    def replay_into(self, block):
        def mk(eng_name):
            def run(E):
                for i, ev in enumerate(self.ev[eng_name]):
                    if ev[0] == "wait":
                        E.wait_ge(ev[1].h, ev[2])
                    else:
                        _, method, args, kwargs, incs = ev
                        try:
                            instr = getattr(E, method)(*args, **kwargs)
                        except Exception as exc:
                            raise RuntimeError(
                                f"replay fail {eng_name}[{i}] {method} "
                                f"args={[getattr(a, 'shape', a) for a in args]}") from exc
                        for s, k in incs:
                            instr = instr.then_inc(s.h, k)
            return run
        block.sync(mk("sync"))
        block.gpsimd(mk("gpsimd"))
        block.vector(mk("vector"))
        block.scalar(mk("scalar"))
        block.tensor(mk("tensor"))

    def n_instr(self):
        return {e: len(self.ev[e]) for e in ENGINES}

    @staticmethod
    def _ap_tensors(args):
        outs, ins = set(), set()
        first = True
        for a in args:
            t = getattr(a, "tensor", None)
            if t is not None:
                (outs if first else ins).add(t.name)
                first = False
        return outs, ins

    def add_engine_drains(self, eng):
        out = []
        reads, writes = set(), set()
        for ev in self.ev[eng]:
            if ev[0] != "op":
                out.append(ev)
                continue
            _, method, args, kwargs, incs = ev
            if method == "dma_start":
                out.append(ev)
                continue
            if method == "drain":
                out.append(ev)
                reads, writes = set(), set()
                continue
            o, i = self._ap_tensors(args)
            if (o & (reads | writes)) or (i & writes):
                out.append(("op", "drain", (), {}, []))
                reads, writes = set(), set()
            if incs:
                out.append(("op", method, args, kwargs, []))
                out.append(("op", "drain", (), {}, incs))
                reads, writes = set(), set()
            else:
                out.append(ev)
                writes |= o
                reads |= i
        self.ev[eng] = out


def build_nc():
    nc = bass.Bass("TRN2", target_bir_lowering=False)

    x_h = nc.dram_tensor("x", [D, TPC], F32, kind="ExternalInput").ap()
    wqkv_h = nc.dram_tensor("wqkv", [D, 3 * D], F32, kind="ExternalInput").ap()
    wout_h = nc.dram_tensor("wout", [D, D], F32, kind="ExternalInput").ap()
    gate_h = nc.dram_tensor("gate", [D, 16], F32, kind="ExternalInput").ap()
    bias_h = nc.dram_tensor("biase", [16], F32, kind="ExternalInput").ap()
    upT_h = nc.dram_tensor("upT", [NEXP, D, 512], F32, kind="ExternalInput").ap()
    downT_h = nc.dram_tensor("downT", [NEXP, 512, D], F32, kind="ExternalInput").ap()
    y_h = nc.dram_tensor("y", [D, TPC], F32, kind="ExternalOutput").ap()
    xmid_h = nc.dram_tensor("xmid", [D, TPC], BF16, kind="Internal").ap()
    srow_h = [nc.dram_tensor(f"srow{i}", [8192], BF16, kind="Internal").ap() for i in range(2)]
    rrow_h = [nc.dram_tensor(f"rrow{i}", [8192], BF16, kind="Internal").ap() for i in range(2)]

    p = Prog()

    with ExitStack() as octx:
        def osb(name, shape, dt):
            return octx.enter_context(nc.sbuf_tensor(name, shape, dt))

        def mksem(name):
            s = Sem(name)
            s.h = octx.enter_context(nc.semaphore(name))
            return s

        sT, sV, sS, sG = mksem("sT"), mksem("sV"), mksem("sS"), mksem("sG")
        sPh = mksem("sPh")
        dInit = mksem("dInit")
        dXin = [mksem(f"dXin{i}") for i in range(2)]
        dXmW = mksem("dXmW")
        dXmR = mksem("dXmR")
        dSrA = [mksem(f"dSrA{i}") for i in range(2)]
        dSrB = [mksem(f"dSrB{i}") for i in range(2)]
        dOut = [mksem(f"dOut{i}") for i in range(2)]
        dWgt = [mksem(f"dWgt{i}") for i in range(3)]

        iota1 = osb("iota1", [128, 128], F32)
        iota2 = osb("iota2", [128, 128], F32)
        identf32 = osb("identf32", [128, 128], F32)
        tri01 = osb("tri01", [128, 128], BF16)
        hselt = osb("hselt", [128, 128], BF16)
        E_sb = osb("E_sb", [16, 2048], BF16)
        ones128 = osb("ones128", [128, 1], BF16)
        onesrow = osb("onesrow", [1, 128], BF16)
        gate_sb = osb("gate_sb", [128, 128], BF16)
        bias_row = osb("bias_row", [1, 16], BF16)
        bias_tok = osb("bias_tok", [128, 16], F32)
        m_row = osb("m_row", [1, 512], F32)
        v_row = osb("v_row", [1, 512], F32)
        t_row = osb("t_row", [1, 512], F32)
        r_bf = osb("r_bf", [1, 512], BF16)
        mr_bf = osb("mr_bf", [1, 512], BF16)
        Rsb = osb("Rsb", [128, 512], BF16)
        MRsb = osb("MRsb", [128, 512], BF16)

        P = [octx.enter_context(nc.psum_tensor(f"P{i}", [128, 512], F32)) for i in range(8)]

        CONSTS = [1e-5, -0.5, 1e-30, 0.5, -0.125, -0.0625, 0.125, -1.0, 2.5, 1e-20]
        cpool = osb("cpool", [128, len(CONSTS)], F32)
        for i, cv in enumerate(CONSTS):
            nc.const_aps.register(F32, cv, cpool[:, i:i + 1])

        p.op("gpsimd", "iota", iota1[:], [[1, 128]], channel_multiplier=-1,
             allow_small_or_imprecise_dtypes=True)
        p.op("gpsimd", "iota", iota2[:], [[-128, 8], [64, 16]], channel_multiplier=-1,
             allow_small_or_imprecise_dtypes=True)
        p.op("gpsimd", "iota", E_sb[:], [[1, 16], [0, 128]], channel_multiplier=-1,
             allow_small_or_imprecise_dtypes=True)
        p.op("gpsimd", "memset", ones128[:], 1.0)
        for i, cv in enumerate(CONSTS):
            p.op("gpsimd", "memset", cpool[:, i:i + 1], cv)
        p.op("gpsimd", "memset", onesrow[:], 1.0, incs=[(sG, 1)])
        g_consts = sG.n
        p.op("gpsimd", "dma_start", gate_sb[:].rearrange("p (c e) -> p c e", e=16),
             gate_h.rearrange("(c p) e -> p c e", p=128), incs=[(dInit, 16)])
        p.op("gpsimd", "dma_start", bias_row[:], bias_h[None, :], incs=[(dInit, 16)])

        p.wait("vector", sG, g_consts)
        p.op("vector", "tensor_single_scalar", tri01[:], iota1[:], 0.0, ALU.is_ge)
        p.op("vector", "tensor_single_scalar", identf32[:], iota1[:], 0.0, ALU.is_equal)
        p.op("vector", "tensor_single_scalar", iota1[:], iota2[:], -64.0, ALU.is_gt)
        p.op("vector", "tensor_single_scalar", iota2[:], iota2[:], 0.0, ALU.is_le)
        p.op("vector", "tensor_single_scalar", E_sb[:], E_sb[:], 0.0, ALU.is_equal)
        p.op("vector", "tensor_mul", hselt[:], iota1[:], iota2[:], incs=[(sV, 1)])
        v_consts = sV.n

        p.wait("tensor", dInit, 64)
        p.wait("tensor", sV, v_consts)
        p.op("tensor", "matmul", P[4][:, 0:16], onesrow[:], bias_row[:],
             start=True, stop=True, incs=[(sT, 1)])
        p.wait("scalar", sT, sT.n)
        p.op("scalar", "copy", bias_tok[:], P[4][:, 0:16], incs=[(sS, 1)])
        s_biastok = sS.n

        sq2_free = [0, 0]
        ln_evac_s = 0
        ln_rows_v = 0

        with ExitStack() as ctx1:
            def sb1(name, shape, dt):
                return ctx1.enter_context(nc.sbuf_tensor(name, shape, dt))

            xin = [sb1(f"xin{i}", [128, NCH * 512], BF16) for i in range(2)]
            xn = sb1("xn", [128, NCH * 512], BF16)
            sq2 = [sb1(f"sq2_{i}", [128, 512], BF16) for i in range(2)]
            qf = sb1("qf", [128, NCH * 512], BF16)
            kroll = [sb1(f"kroll{i}", [128, NCH * 512], BF16) for i in range(2)]
            vroll = [sb1(f"vroll{i}", [128, 4 * 1024], BF16) for i in range(2)]
            probs = [sb1(f"probs{i}", [128, 512], BF16) for i in range(2)]
            aof = sb1("aof", [128, NCH * 512], BF16)
            sums_row = sb1("sums_row", [1, 8192], BF16)
            rrow = sb1("rrow", [1, 8192], BF16)
            spread = sb1("spread", [128, 64], BF16)
            rbsb = [sb1(f"rbsb{i}", [128, 512], BF16) for i in range(2)]
            qmax_sb = sb1("qmax_sb", [16, 1], F32)
            kmax_sb = sb1("kmax_sb", [16, 2], F32)
            u_sb = sb1("u_sb", [16, 1], F32)
            u2_sb = sb1("u2_sb", [16, 1], F32)
            B_sb = sb1("B_sb", [16, 1], F32)
            brow_bf = sb1("brow_bf", [1, 16], BF16)
            bias_all = sb1("bias_all", [128, 16], F32)
            wqkv_sb = sb1("wqkv_sb", [128, NCH * 3 * D], BF16)
            wout_sb = sb1("wout_sb", [128, NCH * D], BF16)

            p.op("gpsimd", "dma_start", wqkv_sb[:].rearrange("p (c n) -> p c n", n=3 * D),
                 wqkv_h.rearrange("(c p) n -> p c n", p=128), incs=[(dInit, 16)])
            p.op("gpsimd", "dma_start", wout_sb[:].rearrange("p (c n) -> p c n", n=D),
                 wout_h.rearrange("(c p) n -> p c n", p=128), incs=[(dInit, 16)])
            p.op("gpsimd", "dma_start", xin[0][:].rearrange("p (c t) -> p c t", t=512),
                 x_h.rearrange("(c p) t -> p c t", p=128)[:, :, 0:512], incs=[(dXin[0], 16)])
            nxin = [1, 0]

            wq = lambda fc, n: 3072 * fc + n

            pr_evac, pr_sem, pr_i = {}, {}, 0
            exp_done, pa_done, ridx = {}, {}, 0
            rb_evac, rb_mult, rb_i = {}, {}, 0
            rb_last_s = 0
            sums_evac_val = 0
            qkv_done_prev = 0
            xadd_done_prev = 0
            xmid_stores = 0

            p.wait("tensor", dInit, 64)
            p.wait("tensor", sS, s_biastok)

            for g in range(NCHUNK):
                c = g % 4
                par = g % 2

                if g + 1 < NCHUNK:
                    np_ = (g + 1) % 2
                    p.wait("gpsimd", sV, xadd_done_prev)
                    p.op("gpsimd", "dma_start",
                         xin[np_][:].rearrange("p (c t) -> p c t", t=512),
                         x_h.rearrange("(c p) t -> p c t", p=128)[:, :, 512 * (g + 1):512 * (g + 2)],
                         incs=[(dXin[np_], 16)])
                    nxin[np_] += 1

                p.wait("vector", dXin[par], 16 * nxin[par])
                p.wait("tensor", dXin[par], 16 * nxin[par])
                if g > 0:
                    p.wait("tensor", sS, sums_evac_val)
                    p.wait("tensor", sS, rb_last_s)
                for fc in range(NCH):
                    p.op("tensor", "matmul", P[3][0:1, :], ones128[:],
                         xin[par][:, 512 * fc:512 * (fc + 1)],
                         start=(fc == 0), stop=(fc == NCH - 1))
                for fc in range(NCH):
                    p.wait("vector", sT, sq2_free[fc % 2])
                    p.op("vector", "tensor_mul", sq2[fc % 2][:],
                         xin[par][:, 512 * fc:512 * (fc + 1)],
                         xin[par][:, 512 * fc:512 * (fc + 1)], incs=[(sV, 1)])
                    p.wait("tensor", sV, sV.n)
                    p.op("tensor", "matmul", P[4][0:1, :], ones128[:], sq2[fc % 2][:],
                         start=(fc == 0), stop=(fc == NCH - 1), incs=[(sT, 1)])
                    sq2_free[fc % 2] = sT.n
                p.wait("vector", sT, sT.n)
                p.op("vector", "tensor_single_scalar", m_row[:], P[3][0:1, :], 1.0 / D, ALU.mult)
                p.op("vector", "tensor_single_scalar", v_row[:], P[4][0:1, :], 1.0 / D, ALU.mult)
                p.op("vector", "tensor_mul", t_row[:], m_row[:], m_row[:])
                p.op("vector", "tensor_sub", v_row[:], v_row[:], t_row[:], incs=[(sV, 1)])
                ln_rows_v = sV.n
                p.wait("scalar", sV, sV.n)
                p.op("scalar", "activation", t_row[:], v_row[:], AF.Ln, bias=1e-5)
                p.op("scalar", "activation", r_bf[:], t_row[:], AF.Exp, scale=-0.5, incs=[(sS, 1)])
                p.wait("vector", sS, sS.n)
                p.op("vector", "tensor_mul", mr_bf[:], m_row[:], r_bf[:], incs=[(sV, 1)])
                p.wait("tensor", sV, sV.n)
                p.op("tensor", "matmul", P[4][:], onesrow[:], r_bf[:], start=True, stop=True)
                p.op("tensor", "matmul", P[5][:], onesrow[:], mr_bf[:], start=True, stop=True,
                     incs=[(sT, 1)])
                p.wait("scalar", sT, sT.n)
                p.op("scalar", "copy", Rsb[:], P[4][:])
                p.op("scalar", "copy", MRsb[:], P[5][:], incs=[(sS, 1)])
                ln_evac_s = sS.n
                p.wait("vector", sS, ln_evac_s)
                p.wait("vector", dXmW, 16 * xmid_stores)
                p.wait("vector", sT, qkv_done_prev)
                xn_v = {}
                for fc in range(NCH):
                    p.wait("vector", sT, sq2_free[fc % 2])
                    p.op("vector", "tensor_mul", sq2[fc % 2][:],
                         xin[par][:, 512 * fc:512 * (fc + 1)], Rsb[:])
                    p.op("vector", "tensor_sub", xn[:, 512 * fc:512 * (fc + 1)],
                         sq2[fc % 2][:], MRsb[:], incs=[(sV, 1)])
                    xn_v[fc] = sV.n
                    sq2_free[fc % 2] = 0

                def mmgroup(dst_ap, oper_fn):
                    nonlocal pr_i
                    bank = 6 + (pr_i % 2)
                    if pr_i >= 2:
                        p.wait("tensor", pr_sem[pr_i - 2], pr_evac[pr_i - 2])
                    for fc in range(NCH):
                        p.wait("tensor", sV, xn_v[fc])
                        lhs, rhs = oper_fn(fc)
                        p.op("tensor", "matmul", P[bank][:], lhs, rhs,
                             start=(fc == 0), stop=(fc == NCH - 1),
                             incs=([(sT, 1)] if fc == NCH - 1 else []))
                    p.wait("scalar", sT, sT.n)
                    p.op("scalar", "copy", dst_ap, P[bank][:], incs=[(sS, 1)])
                    pr_evac[pr_i], pr_sem[pr_i] = sS.n, sS
                    pr_i += 1

                for m in range(NCH):
                    mmgroup(kroll[par][:, 512 * m:512 * (m + 1)],
                            lambda fc, m=m: (wqkv_sb[:, wq(fc, D + 128 * m):wq(fc, D + 128 * m) + 128],
                                             xn[:, 512 * fc:512 * (fc + 1)]))
                for tt in range(4):
                    for half in range(2):
                        mmgroup(vroll[par][:, 1024 * tt + 512 * half:1024 * tt + 512 * (half + 1)],
                                lambda fc, tt=tt, half=half: (
                                    xn[:, 512 * fc + 128 * tt:512 * fc + 128 * (tt + 1)],
                                    wqkv_sb[:, wq(fc, 2 * D + 512 * half):wq(fc, 2 * D + 512 * (half + 1))]))
                for m in range(NCH):
                    mmgroup(qf[:, 512 * m:512 * (m + 1)],
                            lambda fc, m=m: (wqkv_sb[:, wq(fc, 128 * m):wq(fc, 128 * m) + 128],
                                             xn[:, 512 * fc:512 * (fc + 1)]))
                qkv_done_prev = sT.n

                p.wait("tensor", sS, pr_evac[pr_i - 1])
                p.wait("vector", sS, pr_evac[pr_i - 1])
                for which, bank in (("q", 3), ("k", 5)):
                    src = qf if which == "q" else kroll[par]
                    for fc in range(NCH):
                        p.wait("vector", sT, sq2_free[fc % 2])
                        p.op("vector", "tensor_mul", sq2[fc % 2][:],
                             src[:, 512 * fc:512 * (fc + 1)], src[:, 512 * fc:512 * (fc + 1)],
                             incs=[(sV, 1)])
                        p.wait("tensor", sV, sV.n)
                        p.op("tensor", "matmul", P[bank][0:16, :],
                             hselt[:, 16 * fc:16 * (fc + 1)], sq2[fc % 2][:],
                             start=(fc == 0), stop=(fc == NCH - 1), incs=[(sT, 1)])
                        sq2_free[fc % 2] = sT.n
                p.wait("vector", sT, sT.n)
                p.op("vector", "reduce_max", qmax_sb[:], P[3][0:16, :], AX.X)
                if c > 0:
                    p.op("vector", "reduce_max", kmax_sb[:, par:par + 1], P[5][0:16, :], AX.X)
                    p.op("vector", "tensor_max", u_sb[:], kmax_sb[:, 0:1], kmax_sb[:, 1:2],
                         incs=[(sV, 1)])
                    p.wait("scalar", sV, sV.n)
                    p.op("scalar", "add", u2_sb[:], u_sb[:], qmax_sb[:], incs=[(sS, 1)])
                else:
                    p.op("vector", "reduce_max", kmax_sb[:, par:par + 1], P[5][0:16, :], AX.X,
                         incs=[(sV, 1)])
                    p.wait("scalar", sV, sV.n)
                    p.op("scalar", "add", u2_sb[:], kmax_sb[:, par:par + 1], qmax_sb[:],
                         incs=[(sS, 1)])
                p.wait("tensor", sS, sS.n)
                p.op("tensor", "transpose", P[4][0:1, 0:16], u2_sb[:], identf32[0:16, 0:16],
                     incs=[(sT, 1)])
                p.wait("scalar", sT, sT.n)
                p.op("scalar", "copy", brow_bf[:], P[4][0:1, 0:16], incs=[(sS, 1)])
                p.wait("tensor", sS, sS.n)
                p.op("tensor", "matmul", P[4][:, 0:16], onesrow[:], brow_bf[:],
                     start=True, stop=True, incs=[(sT, 1)])
                p.wait("scalar", sT, sT.n)
                p.op("scalar", "activation", bias_all[:], P[4][:, 0:16], AF.Copy,
                     scale=-0.0625, incs=[(sS, 1)])

                for h in range(NH):
                    fch, off = h // 2, 64 * (h % 2)
                    kts = ([("prev", kt) for kt in range(4)] if c > 0 else []) \
                        + [("cur", kt) for kt in range(4)]
                    nkts = len(kts)
                    for idx, (which, kt) in enumerate(kts):
                        rb = ridx % 2
                        qlo = 0 if which == "prev" else 128 * kt
                        ksrc = kroll[(g - 1) % 2] if which == "prev" else kroll[par]
                        vsrc = vroll[(g - 1) % 2] if which == "prev" else vroll[par]
                        if ridx >= 2:
                            p.wait("tensor", sS, exp_done[ridx - 2])
                        p.op("tensor", "matmul", P[rb][:, qlo:512],
                             ksrc[off:off + 64, 512 * fch + 128 * kt:512 * fch + 128 * (kt + 1)],
                             qf[off:off + 64, 512 * fch + qlo:512 * (fch + 1)],
                             start=True, stop=True, incs=[(sT, 1)])
                        p.wait("scalar", sT, sT.n)
                        if ridx >= 2:
                            p.wait("scalar", sT, pa_done[ridx - 2])
                        p.op("scalar", "activation", probs[rb][:, qlo:512], P[rb][:, qlo:512],
                             AF.Exp, bias=bias_all[:, h:h + 1], scale=0.125, incs=[(sS, 1)])
                        exp_done[ridx] = sS.n
                        if which == "cur":
                            p.wait("vector", sS, exp_done[ridx])
                            p.op("vector", "tensor_mul", probs[rb][:, qlo:qlo + 128],
                                 probs[rb][:, qlo:qlo + 128], tri01[:], incs=[(sV, 1)])
                            p.wait("tensor", sV, sV.n)
                        p.wait("tensor", sS, exp_done[ridx])
                        last = (idx == nkts - 1)
                        p.op("tensor", "matmul", P[3][0:1, qlo:512], ones128[:],
                             probs[rb][:, qlo:512], start=(idx == 0), stop=last)
                        p.op("tensor", "matmul", P[2][off:off + 64, qlo:512],
                             vsrc[:, 1024 * kt + 64 * h:1024 * kt + 64 * (h + 1)],
                             probs[rb][:, qlo:512], start=(idx == 0), stop=last,
                             incs=[(sT, 1)])
                        pa_done[ridx] = sT.n
                        ridx += 1
                    p.wait("scalar", sT, sT.n)
                    p.op("scalar", "copy", sums_row[0:1, 512 * h:512 * (h + 1)], P[3][0:1, :])
                    p.op("scalar", "copy", aof[off:off + 64, 512 * fch:512 * (fch + 1)],
                         P[2][off:off + 64, :], incs=[(sS, 1)])
                    sums_evac_val = sS.n
                    p.wait("tensor", sS, sums_evac_val)

                p.wait("sync", sS, sums_evac_val)
                nA = 32 * (g // 2)
                p.op("sync", "dma_start", srow_h[par][None, :], sums_row[:],
                     incs=[(dSrA[par], 16)])
                p.wait("sync", dSrA[par], nA + 16)
                p.op("sync", "dma_start", spread[:],
                     srow_h[par].rearrange("(p j) -> p j", p=128), incs=[(dSrA[par], 16)])
                p.wait("vector", dSrA[par], nA + 32)
                p.op("vector", "reciprocal", spread[:], spread[:], incs=[(sV, 1)])
                p.wait("sync", sV, sV.n)
                nB = 32 * (g // 2)
                p.op("sync", "dma_start", rrow_h[par].rearrange("(p j) -> p j", p=128),
                     spread[:], incs=[(dSrB[par], 16)])
                p.wait("sync", dSrB[par], nB + 16)
                p.op("sync", "dma_start", rrow[:], rrow_h[par][None, :], incs=[(dSrB[par], 16)])
                p.wait("tensor", dSrB[par], nB + 32)
                for h in range(NH):
                    fch, off = h // 2, 64 * (h % 2)
                    bank = 4 + (rb_i % 2)
                    if rb_i >= 2:
                        p.wait("tensor", sS, rb_evac[rb_i - 2])
                    p.op("tensor", "matmul", P[bank][off:off + 64, :], onesrow[0:1, 0:64],
                         rrow[0:1, 512 * h:512 * (h + 1)], start=True, stop=True, incs=[(sT, 1)])
                    p.wait("scalar", sT, sT.n)
                    if rb_i >= 2:
                        p.wait("scalar", sV, rb_mult[rb_i - 2])
                    p.op("scalar", "copy", rbsb[rb_i % 2][off:off + 64, :],
                         P[bank][off:off + 64, :], incs=[(sS, 1)])
                    rb_evac[rb_i] = sS.n
                    p.wait("vector", sS, rb_evac[rb_i])
                    p.op("vector", "tensor_mul", aof[off:off + 64, 512 * fch:512 * (fch + 1)],
                         aof[off:off + 64, 512 * fch:512 * (fch + 1)],
                         rbsb[rb_i % 2][off:off + 64, :], incs=[(sV, 1)])
                    rb_mult[rb_i] = sV.n
                    rb_i += 1
                rb_last_s = rb_evac[rb_i - 1]
                v_aof = sV.n

                p.wait("tensor", sV, v_aof)
                for m in range(NCH):
                    bank = 6 + (pr_i % 2)
                    if pr_i >= 2:
                        p.wait("tensor", pr_sem[pr_i - 2], pr_evac[pr_i - 2])
                    for fc in range(NCH):
                        p.op("tensor", "matmul", P[bank][:],
                             wout_sb[:, 1024 * fc + 128 * m:1024 * fc + 128 * (m + 1)],
                             aof[:, 512 * fc:512 * (fc + 1)],
                             start=(fc == 0), stop=(fc == NCH - 1),
                             incs=([(sT, 1)] if fc == NCH - 1 else []))
                    p.wait("vector", sT, sT.n)
                    p.op("vector", "tensor_add", xn[:, 512 * m:512 * (m + 1)], P[bank][:],
                         xin[par][:, 512 * m:512 * (m + 1)], incs=[(sV, 1)])
                    pr_evac[pr_i], pr_sem[pr_i] = sV.n, sV
                    pr_i += 1
                xadd_done_prev = sV.n
                p.wait("sync", sV, xadd_done_prev)
                p.op("sync", "dma_start",
                     xmid_h.rearrange("(c p) t -> p c t", p=128)[:, :, 512 * g:512 * (g + 1)],
                     xn[:].rearrange("p (c t) -> p c t", t=512), incs=[(dXmW, 16)])
                xmid_stores += 1

            p.wait("sync", dXmW, 16 * xmid_stores)
            p.wait("gpsimd", dXin[0], 16 * nxin[0])
            p.wait("gpsimd", dXin[1], 16 * nxin[1])
            p.op("vector", "tensor_copy", t_row[:], m_row[:], incs=[(sPh, 1)])
            p.op("scalar", "copy", v_row[:], m_row[:], incs=[(sPh, 1)])
            for e in ("sync", "gpsimd", "tensor"):
                p.op(e, "nop", incs=[(sPh, 1)])
            for e in ENGINES:
                p.wait(e, sPh, 5)

        with ExitStack() as ctx2:
            def sb2(name, shape, dt):
                return ctx2.enter_context(nc.sbuf_tensor(name, shape, dt))

            xres = sb2("xres", [128, NCH * TPC], BF16)
            xn2 = sb2("xn2", [128, NCH * TPC], BF16)
            wb = [sb2(f"wb{i}", [128, 4096], BF16) for i in range(3)]
            H = [sb2(f"H{i}", [128, 512], BF16) for i in range(4)]
            h1 = [sb2(f"h1_{i}", [128, 512], BF16) for i in range(2)]
            cwb = [sb2(f"cwb{i}", [128, 512], BF16) for i in range(2)]
            sq2b = h1
            scorestok = sb2("scorestok", [128, 512], F32)
            sctok = sb2("sctok", [128, 512], F32)
            m32 = sb2("m32", [128, 32], F32)
            eqb = sb2("eqb", [128, 512], BF16)
            pfx = [sb2(f"pfx{i}", [128, 512], BF16) for i in range(2)]
            selb = sb2("selb", [128, 512], BF16)
            selsum = sb2("selsum", [128, 512], BF16)
            wtok = sb2("wtok", [128, 512], F32)
            exl = wtok
            wsum = sb2("wsum", [128, 32], F32)
            wsum2 = sb2("wsum2", [128, 32], F32)
            winv = sb2("winv", [128, 32], F32)
            gp6 = sb2("gp6", [128, 768], F32)
            gtree = sb2("gtree", [128, 512], F32)
            gcmp = gp6
            granks = gtree
            gsc = sb2("gsc", [128, 128], F32)
            gmask = sb2("gmask", [128, 128], F32)
            cmb_f = sb2("cmb_f", [16, 4096], BF16)
            stage = [sb2(f"stage{i}", [128, 512], F32) for i in range(2)]

            p.op("sync", "dma_start", xres[:].rearrange("p (c t) -> p c t", t=TPC),
                 xmid_h.rearrange("(c p) t -> p c t", p=128), incs=[(dXmR, 16)])
            p.wait("vector", dXmR, 16)
            p.wait("tensor", dXmR, 16)

            def load_w(slot, e, is_up):
                src = upT_h[e] if is_up else downT_h[e]
                inner = 512 if is_up else D
                p.op("gpsimd", "dma_start",
                     wb[slot][:].rearrange("p (c i) -> p c i", i=inner),
                     src.rearrange("(c p) i -> p c i", p=128), incs=[(dWgt[slot], 16)])

            slot_loads = [0, 0, 0]
            su = [0] * NEXP
            sd = [0] * NEXP
            su[0], sd[0], su[1] = 0, 1, 2
            for e in range(1, NEXP):
                if e >= 2:
                    su[e] = sd[e - 2]
                sd[e] = su[e - 1]
            load_w(0, 0, True); slot_loads[0] += 1
            load_w(1, 0, False); slot_loads[1] += 1
            load_w(2, 1, True); slot_loads[2] += 1

            logit_mm_done = 0
            sq2_free = [0, 0]
            for tau in range(NTAU):
                if tau > 0:
                    p.wait("tensor", sS, ln_evac_s)
                    p.wait("tensor", sV, ln_rows_v)
                for fc in range(NCH):
                    p.op("tensor", "matmul", P[3][0:1, :], ones128[:],
                         xres[:, TPC * fc + 512 * tau:TPC * fc + 512 * (tau + 1)],
                         start=(fc == 0), stop=(fc == NCH - 1))
                for fc in range(NCH):
                    p.wait("vector", sT, sq2_free[fc % 2])
                    p.op("vector", "tensor_mul", sq2b[fc % 2][:],
                         xres[:, TPC * fc + 512 * tau:TPC * fc + 512 * (tau + 1)],
                         xres[:, TPC * fc + 512 * tau:TPC * fc + 512 * (tau + 1)], incs=[(sV, 1)])
                    p.wait("tensor", sV, sV.n)
                    p.op("tensor", "matmul", P[4][0:1, :], ones128[:], sq2b[fc % 2][:],
                         start=(fc == 0), stop=(fc == NCH - 1), incs=[(sT, 1)])
                    sq2_free[fc % 2] = sT.n
                p.wait("vector", sT, sT.n)
                p.op("vector", "tensor_single_scalar", m_row[:], P[3][0:1, :], 1.0 / D, ALU.mult)
                p.op("vector", "tensor_single_scalar", v_row[:], P[4][0:1, :], 1.0 / D, ALU.mult)
                p.op("vector", "tensor_mul", t_row[:], m_row[:], m_row[:])
                p.op("vector", "tensor_sub", v_row[:], v_row[:], t_row[:], incs=[(sV, 1)])
                ln_rows_v = sV.n
                p.wait("scalar", sV, sV.n)
                p.op("scalar", "activation", t_row[:], v_row[:], AF.Ln, bias=1e-5)
                p.op("scalar", "activation", r_bf[:], t_row[:], AF.Exp, scale=-0.5, incs=[(sS, 1)])
                p.wait("vector", sS, sS.n)
                p.op("vector", "tensor_mul", mr_bf[:], m_row[:], r_bf[:], incs=[(sV, 1)])
                p.wait("tensor", sV, sV.n)
                p.op("tensor", "matmul", P[4][:], onesrow[:], r_bf[:], start=True, stop=True)
                p.op("tensor", "matmul", P[5][:], onesrow[:], mr_bf[:], start=True, stop=True,
                     incs=[(sT, 1)])
                p.wait("scalar", sT, sT.n)
                p.op("scalar", "copy", Rsb[:], P[4][:])
                p.op("scalar", "copy", MRsb[:], P[5][:], incs=[(sS, 1)])
                ln_evac_s = sS.n
                p.wait("vector", sS, ln_evac_s)
                for fc in range(NCH):
                    p.wait("vector", sT, sq2_free[fc % 2])
                    p.op("vector", "tensor_mul", sq2b[fc % 2][:],
                         xres[:, TPC * fc + 512 * tau:TPC * fc + 512 * (tau + 1)], Rsb[:])
                    p.op("vector", "tensor_sub",
                         xn2[:, TPC * fc + 512 * tau:TPC * fc + 512 * (tau + 1)],
                         sq2b[fc % 2][:], MRsb[:], incs=[(sV, 1)])
                    sq2_free[fc % 2] = 0
                xn2_tau_v = sV.n
                p.wait("tensor", sV, xn2_tau_v)
                for tt in range(4):
                    for fc in range(NCH):
                        p.op("tensor", "matmul",
                             P[6][:, 64 * tau + 16 * tt:64 * tau + 16 * (tt + 1)],
                             xn2[:, TPC * fc + 512 * tau + 128 * tt:TPC * fc + 512 * tau + 128 * (tt + 1)],
                             gate_sb[:, 16 * fc:16 * (fc + 1)],
                             start=(fc == 0), stop=(fc == NCH - 1),
                             incs=([(sT, 1)] if fc == NCH - 1 else []))
                logit_mm_done = sT.n

            p.wait("scalar", sT, logit_mm_done)
            for j in range(4):
                p.op("scalar", "activation", exl[:, 128 * j:128 * (j + 1)],
                     P[6][:, 128 * j:128 * (j + 1)], AF.Exp, scale=-1.0,
                     incs=([(sS, 1)] if j == 3 else []))
            s_exl = sS.n
            p.wait("vector", sS, s_exl)
            p.op("vector", "tensor_single_scalar", exl[:], exl[:], 1.0, ALU.add)
            p.op("vector", "reciprocal", scorestok[:], exl[:])
            sct3 = sctok[:].rearrange("p (t e) -> p t e", e=16)
            sco3 = scorestok[:].rearrange("p (t e) -> p t e", e=16)
            p.op("vector", "tensor_tensor", sct3, sco3,
                 bias_tok[:, None, :].broadcast_to([128, 32, 16]), ALU.add)

            scg = sctok[:].rearrange("p (gg e) -> p gg e", e=4)
            pairs = [(0, 1), (0, 2), (0, 3), (1, 2), (1, 3), (2, 3)]
            for i, (a, b) in enumerate(pairs):
                p.op("vector", "tensor_tensor", gp6[:, 128 * i:128 * (i + 1), None],
                     scg[:, :, a:a + 1], scg[:, :, b:b + 1], ALU.add)
            p.op("vector", "tensor_max", gtree[:, 0:128], gp6[:, 0:128], gp6[:, 128:256])
            p.op("vector", "tensor_max", gtree[:, 128:256], gp6[:, 256:384], gp6[:, 384:512])
            p.op("vector", "tensor_max", gtree[:, 256:384], gp6[:, 512:640], gp6[:, 640:768])
            p.op("vector", "tensor_max", gtree[:, 384:512], gtree[:, 0:128], gtree[:, 128:256])
            p.op("vector", "tensor_copy", pfx[0][:], scorestok[:])
            p.op("vector", "tensor_max", gsc[:], gtree[:, 384:512], gtree[:, 256:384])
            p.op("vector", "tensor_copy", pfx[1][:], scorestok[:])
            g3 = gsc[:].rearrange("p (t g) -> p t g", g=4)
            gm3 = gmask[:].rearrange("p (t g) -> p t g", g=4)
            ci = 0
            for j in range(4):
                for mth in range(4):
                    if mth == j:
                        continue
                    opc = ALU.is_ge if mth < j else ALU.is_gt
                    p.op("vector", "tensor_tensor", gcmp[:, 32 * ci:32 * (ci + 1), None],
                         g3[:, :, mth:mth + 1], g3[:, :, j:j + 1], opc)
                    ci += 1
            for j in range(4):
                p.op("vector", "tensor_add", granks[:, 32 * j:32 * (j + 1)],
                     gcmp[:, 96 * j:96 * j + 32], gcmp[:, 96 * j + 32:96 * j + 64])
            for j in range(4):
                p.op("vector", "tensor_add", granks[:, 32 * j:32 * (j + 1)],
                     granks[:, 32 * j:32 * (j + 1)], gcmp[:, 96 * j + 64:96 * j + 96])
            for j in range(4):
                p.op("vector", "tensor_single_scalar", gm3[:, :, j:j + 1],
                     granks[:, 32 * j:32 * (j + 1), None], 1.5, ALU.is_le)
            p.op("vector", "tensor_add", gtree[:, 128:256], gsc[:], gsc[:])
            p.op("vector", "tensor_add", gtree[:, 128:256], gsc[:], gsc[:])
            p.op("vector", "tensor_tensor", scg, scg,
                 gmask[:, :, None].broadcast_to([128, 128, 4]), ALU.mult)

            eq3 = eqb[:].rearrange("p (t e) -> p t e", e=16)
            for r in range(4):
                p.op("vector", "reduce_max", m32[:], sct3, AX.X)
                p.op("vector", "tensor_tensor", eq3, sct3,
                     m32[:, :, None].broadcast_to([128, 32, 16]), ALU.is_ge)
                cur = eqb
                for si, s in enumerate((1, 2, 4, 8)):
                    nxt = pfx[si % 2]
                    cv = cur[:].rearrange("p (t e) -> p t e", e=16)
                    nv = nxt[:].rearrange("p (t e) -> p t e", e=16)
                    p.op("vector", "tensor_copy", nv[:, :, 0:s], cv[:, :, 0:s])
                    p.op("vector", "tensor_tensor", nv[:, :, s:16], cv[:, :, s:16],
                         cv[:, :, 0:16 - s], ALU.add)
                    cur = nxt
                p.op("vector", "tensor_single_scalar", gp6[:, 0:512], cur[:], 1.0,
                     ALU.is_equal)
                p.op("vector", "tensor_mul", selb[:], eqb[:], gp6[:, 0:512])
                if r == 0:
                    p.op("vector", "tensor_copy", selsum[:], selb[:])
                else:
                    p.op("vector", "tensor_add", selsum[:], selsum[:], selb[:])
                if r < 3:
                    p.op("vector", "scalar_tensor_tensor", sctok[:], selb[:], -1e30,
                         sctok[:], ALU.mult, ALU.add)
            p.op("vector", "tensor_mul", wtok[:], scorestok[:], selsum[:])
            w3 = wtok[:].rearrange("p (t e) -> p t e", e=16)
            p.op("vector", "reduce_sum", wsum[:], w3, AX.X)
            p.op("vector", "tensor_scalar_add", wsum2[:], wsum[:],
                 cpool[:, CONSTS.index(1e-20):CONSTS.index(1e-20) + 1])
            p.op("vector", "tensor_add", gp6[:, 0:512], scorestok[:], scorestok[:])
            p.op("vector", "tensor_add", gp6[:, 0:512], scorestok[:], scorestok[:])
            p.op("vector", "reciprocal", winv[:], wsum2[:])
            p.op("vector", "tensor_add", gp6[:, 0:512], scorestok[:], scorestok[:])
            p.op("vector", "tensor_tensor", w3, w3,
                 winv[:, :, None].broadcast_to([128, 32, 16]), ALU.mult, incs=[(sV, 1)])
            v_comb = sV.n

            p.wait("tensor", sV, v_comb)
            p.wait("tensor", sS, ln_evac_s)
            cmb_evac = {}
            for tt in range(32):
                bank = 4 + (tt % 2)
                if tt >= 2:
                    p.wait("tensor", sS, cmb_evac[tt - 2])
                p.op("tensor", "transpose", P[bank][0:16, 0:128],
                     wtok[:, 16 * tt:16 * (tt + 1)], identf32[:], incs=[(sT, 1)])
                p.wait("scalar", sT, sT.n)
                p.op("scalar", "copy", cmb_f[:, 128 * tt:128 * (tt + 1)],
                     P[bank][0:16, 0:128], incs=[(sS, 1)])
                cmb_evac[tt] = sS.n
            s_cmb = sS.n

            H_free = 0
            upp_free = [0, 0]
            dwp_free = [0, 0]
            cwb_free = [0, 0]
            cwb_s = [0, 0]
            h1_free = [0, 0]
            stage_dma = [0, 0]

            for e in range(NEXP):
                p.wait("tensor", dWgt[su[e]], 16 * slot_loads[su[e]])
                first_dn_wait = 16 * slot_loads[sd[e]]
                for tau in range(NTAU):
                    if e < 16:
                        cb = tau % 2
                        p.wait("tensor", sV, cwb_free[cb])
                        p.wait("tensor", sS, s_cmb)
                        p.op("tensor", "matmul", P[6 + cb][:], E_sb[:, 128 * e:128 * (e + 1)],
                             cmb_f[:, 512 * tau:512 * (tau + 1)], start=True, stop=True,
                             incs=[(sT, 1)])
                        p.wait("scalar", sT, sT.n)
                        p.op("scalar", "activation", cwb[cb][:], P[6 + cb][:], AF.Copy,
                             scale=2.5, incs=[(sS, 1)])
                        cwb_s[cb] = sS.n
                    h_last_v = 0
                    for m in range(4):
                        ub = m % 2
                        p.wait("tensor", sS, upp_free[ub])
                        for kc in range(NCH):
                            p.op("tensor", "matmul", P[ub][:],
                                 wb[su[e]][:, 512 * kc + 128 * m:512 * kc + 128 * (m + 1)],
                                 xn2[:, TPC * kc + 512 * tau:TPC * kc + 512 * (tau + 1)],
                                 start=(kc == 0), stop=(kc == NCH - 1),
                                 incs=([(sT, 1)] if kc == NCH - 1 else []))
                        p.wait("scalar", sT, sT.n)
                        p.wait("scalar", sV, h1_free[ub])
                        p.op("scalar", "activation", h1[ub][:], P[ub][:], AF.Relu,
                             incs=[(sS, 1)])
                        upp_free[ub] = sS.n
                        p.wait("vector", sS, upp_free[ub])
                        p.wait("vector", sT, H_free)
                        if e < 16:
                            p.wait("vector", sS, cwb_s[tau % 2])
                            p.op("vector", "tensor_mul", H[m][:], h1[ub][:], cwb[tau % 2][:])
                            p.op("vector", "tensor_mul", H[m][:], H[m][:], h1[ub][:],
                                 incs=[(sV, 1)])
                        else:
                            p.op("vector", "tensor_mul", H[m][:], h1[ub][:], h1[ub][:],
                                 incs=[(sV, 1)])
                        h1_free[ub] = sV.n
                        h_last_v = sV.n
                    if e < 16:
                        cwb_free[tau % 2] = h_last_v
                    p.wait("tensor", sV, h_last_v)
                    p.wait("tensor", dWgt[sd[e]], first_dn_wait)
                    for n in range(NCH):
                        db = n % 2
                        p.wait("tensor", sV, dwp_free[db])
                        for kc in range(4):
                            p.op("tensor", "matmul", P[2 + db][:],
                                 wb[sd[e]][:, 1024 * kc + 128 * n:1024 * kc + 128 * (n + 1)],
                                 H[kc][:], start=(kc == 0), stop=(kc == 3),
                                 incs=([(sT, 1)] if kc == 3 else []))
                        p.wait("vector", sT, sT.n)
                        xslice = xres[:, TPC * n + 512 * tau:TPC * n + 512 * (tau + 1)]
                        if e < NEXP - 1:
                            p.op("vector", "tensor_add", xslice, P[2 + db][:], xslice,
                                 incs=[(sV, 1)])
                        else:
                            sp = n % 2
                            p.wait("vector", dOut[sp], 16 * stage_dma[sp])
                            p.op("vector", "tensor_add", stage[sp][:], P[2 + db][:], xslice,
                                 incs=[(sV, 1)])
                            p.wait("sync", sV, sV.n)
                            p.op("sync", "dma_start",
                                 y_h[128 * n:128 * (n + 1), 512 * tau:512 * (tau + 1)],
                                 stage[sp][:], incs=[(dOut[sp], 16)])
                            stage_dma[sp] += 1
                        dwp_free[db] = sV.n
                    H_free = sT.n
                if e + 1 < NEXP:
                    p.wait("gpsimd", sT, H_free)
                    load_w(su[e], e + 1, False)
                    slot_loads[su[e]] += 1
                if e + 2 < NEXP:
                    p.wait("gpsimd", sT, H_free)
                    load_w(sd[e], e + 2, True)
                    slot_loads[sd[e]] += 1

            p.wait("sync", dOut[0], 16 * stage_dma[0])
            p.wait("sync", dOut[1], 16 * stage_dma[1])

        for _e in ("vector", "scalar"):
            p.add_engine_drains(_e)
        p.simulate()
        with nc.allow_low_precision(reason="bf16 residual stream / exact small-int masks; "
                                           "error budget analyzed vs rel-L2 4.8e-2 bar"):
            with nc.Block() as block:
                p.replay_into(block)
    return nc, p


_NC_CACHE = {}


def _get_nc():
    if "nc" not in _NC_CACHE:
        _NC_CACHE["nc"] = build_nc()
    return _NC_CACHE["nc"]


def kernel(L0_attn_x, L0_attn_w_qkv, L0_attn_w_out, L0_ffn_gate_w, L0_ffn_bias_e,
           L0_ffn_up_w, L0_ffn_down_w, L0_ffn_shared_up, L0_ffn_shared_down):
    x = np.asarray(L0_attn_x, dtype=np.float32)
    wqkv = np.ascontiguousarray(np.asarray(L0_attn_w_qkv, dtype=np.float32))
    wout = np.ascontiguousarray(np.asarray(L0_attn_w_out, dtype=np.float32))
    gate = np.ascontiguousarray(np.asarray(L0_ffn_gate_w, dtype=np.float32).T)
    biase = np.ascontiguousarray(np.asarray(L0_ffn_bias_e, dtype=np.float32))
    up_w = np.asarray(L0_ffn_up_w, dtype=np.float32)
    down_w = np.asarray(L0_ffn_down_w, dtype=np.float32)
    sup = np.asarray(L0_ffn_shared_up, dtype=np.float32)
    sdn = np.asarray(L0_ffn_shared_down, dtype=np.float32)

    upT = np.ascontiguousarray(np.concatenate(
        [up_w.transpose(0, 2, 1), sup.reshape(4, 512, D).transpose(0, 2, 1)], axis=0))
    downT = np.ascontiguousarray(np.concatenate(
        [down_w.transpose(0, 2, 1), np.ascontiguousarray(sdn.T).reshape(4, 512, D)], axis=0))

    nc, _ = _get_nc()
    in_maps = []
    for ci in range(NCORES):
        xc = np.ascontiguousarray(x[2 * ci:2 * ci + 2].reshape(TPC, D).T)
        in_maps.append({"x": xc, "wqkv": wqkv, "wout": wout, "gate": gate,
                        "biase": biase, "upT": upT, "downT": downT})
    res = run_bass_kernel_spmd(nc, in_maps, core_ids=list(range(NCORES)))
    y = np.empty((16, 2048, 1024), dtype=np.float32)
    for ci in range(NCORES):
        y[2 * ci:2 * ci + 2] = res.results[ci]["y"].T.reshape(2, 2048, D)
    return y


if __name__ == "__main__":
    nc, p = build_nc()
    print("instructions:", p.n_instr())


# version 10
# speedup vs baseline: 1.2677x; 1.2451x over previous
import numpy as np
from contextlib import ExitStack

import concourse.bass as bass
import concourse.mybir as mybir
from concourse.bass_utils import run_bass_kernel_spmd

F32 = mybir.dt.float32
BF16 = mybir.dt.bfloat16
AF = mybir.ActivationFunctionType
ALU = mybir.AluOpType
AX = mybir.AxisListType

NCORES = 8
D = 1024
NCH = 8
NH = 16
NCHUNK = 8
TPC = 4096
NTAU = 8
NEXP = 20
ENGINES = ("sync", "gpsimd", "vector", "scalar", "tensor")


class Sem:
    __slots__ = ("name", "n", "h")

    def __init__(self, name):
        self.name, self.n, self.h = name, 0, None


class Prog:
    def __init__(self):
        self.ev = {e: [] for e in ENGINES}

    def op(self, eng, method, *args, incs=(), **kwargs):
        incs = list(incs)
        self.ev[eng].append(("op", method, args, kwargs, incs))
        for s, k in incs:
            s.n += k

    def wait(self, eng, sem, val):
        if val > 0:
            self.ev[eng].append(("wait", sem, val))

    def simulate(self):
        ptr = {e: 0 for e in ENGINES}
        vals = {}
        progress = True
        while progress:
            progress = False
            for e in ENGINES:
                while ptr[e] < len(self.ev[e]):
                    ev = self.ev[e][ptr[e]]
                    if ev[0] == "wait":
                        if vals.get(id(ev[1]), 0) < ev[2]:
                            break
                    else:
                        for s, k in ev[4]:
                            vals[id(s)] = vals.get(id(s), 0) + k
                    ptr[e] += 1
                    progress = True
        stuck = [e for e in ENGINES if ptr[e] < len(self.ev[e])]
        if stuck:
            lines = []
            for e in stuck:
                ev = self.ev[e][ptr[e]]
                lines.append(f"  {e}@{ptr[e]}/{len(self.ev[e])}: wait {ev[1].name} >= {ev[2]}"
                             f" (have {vals.get(id(ev[1]), 0)}, final {ev[1].n})")
            raise RuntimeError("SCHEDULE DEADLOCK:\n" + "\n".join(lines))

    def replay_into(self, block):
        def mk(eng_name):
            def run(E):
                for i, ev in enumerate(self.ev[eng_name]):
                    if ev[0] == "wait":
                        E.wait_ge(ev[1].h, ev[2])
                    else:
                        _, method, args, kwargs, incs = ev
                        try:
                            instr = getattr(E, method)(*args, **kwargs)
                        except Exception as exc:
                            raise RuntimeError(
                                f"replay fail {eng_name}[{i}] {method} "
                                f"args={[getattr(a, 'shape', a) for a in args]}") from exc
                        for s, k in incs:
                            instr = instr.then_inc(s.h, k)
            return run
        block.sync(mk("sync"))
        block.gpsimd(mk("gpsimd"))
        block.vector(mk("vector"))
        block.scalar(mk("scalar"))
        block.tensor(mk("tensor"))

    def n_instr(self):
        return {e: len(self.ev[e]) for e in ENGINES}

    @staticmethod
    def _ap_tensors(args):
        outs, ins = set(), set()
        first = True
        for a in args:
            t = getattr(a, "tensor", None)
            if t is not None:
                (outs if first else ins).add(t.name)
                first = False
        return outs, ins

    def add_engine_drains(self, eng):
        out = []
        reads, writes = set(), set()
        for ev in self.ev[eng]:
            if ev[0] != "op":
                out.append(ev)
                continue
            _, method, args, kwargs, incs = ev
            if method == "dma_start":
                out.append(ev)
                continue
            if method == "drain":
                out.append(ev)
                reads, writes = set(), set()
                continue
            o, i = self._ap_tensors(args)
            if (o & (reads | writes)) or (i & writes):
                out.append(("op", "drain", (), {}, []))
                reads, writes = set(), set()
            if incs:
                out.append(("op", method, args, kwargs, []))
                out.append(("op", "drain", (), {}, incs))
                reads, writes = set(), set()
            else:
                out.append(ev)
                writes |= o
                reads |= i
        self.ev[eng] = out


def build_nc():
    nc = bass.Bass("TRN2", target_bir_lowering=False)

    x_h = nc.dram_tensor("x", [D, TPC], F32, kind="ExternalInput").ap()
    wqkv_h = nc.dram_tensor("wqkv", [D, 3 * D], F32, kind="ExternalInput").ap()
    wout_h = nc.dram_tensor("wout", [D, D], F32, kind="ExternalInput").ap()
    gate_h = nc.dram_tensor("gate", [D, 16], F32, kind="ExternalInput").ap()
    bias_h = nc.dram_tensor("biase", [16], F32, kind="ExternalInput").ap()
    upT_h = nc.dram_tensor("upT", [NEXP, D, 512], F32, kind="ExternalInput").ap()
    downT_h = nc.dram_tensor("downT", [NEXP, 512, D], F32, kind="ExternalInput").ap()
    y_h = nc.dram_tensor("y", [D, TPC], F32, kind="ExternalOutput").ap()
    xmid_h = nc.dram_tensor("xmid", [D, TPC], BF16, kind="Internal").ap()
    srow_h = [nc.dram_tensor(f"srow{i}", [8192], BF16, kind="Internal").ap() for i in range(2)]
    rrow_h = [nc.dram_tensor(f"rrow{i}", [8192], BF16, kind="Internal").ap() for i in range(2)]

    p = Prog()

    with ExitStack() as octx:
        def osb(name, shape, dt):
            return octx.enter_context(nc.sbuf_tensor(name, shape, dt))

        def mksem(name):
            s = Sem(name)
            s.h = octx.enter_context(nc.semaphore(name))
            return s

        sT, sV, sS, sG = mksem("sT"), mksem("sV"), mksem("sS"), mksem("sG")
        sPh = mksem("sPh")
        dInit = mksem("dInit")
        dXin = [mksem(f"dXin{i}") for i in range(2)]
        dXmW = mksem("dXmW")
        dXmR = mksem("dXmR")
        dSrA = [mksem(f"dSrA{i}") for i in range(2)]
        dSrB = [mksem(f"dSrB{i}") for i in range(2)]
        dOut = [mksem(f"dOut{i}") for i in range(2)]
        dWgt = [mksem(f"dWgt{i}") for i in range(3)]

        iota1 = osb("iota1", [128, 128], F32)
        iota2 = osb("iota2", [128, 128], F32)
        identf32 = osb("identf32", [128, 128], F32)
        tri01 = osb("tri01", [128, 128], BF16)
        hselt = osb("hselt", [128, 128], BF16)
        E_sb = osb("E_sb", [16, 2048], BF16)
        ones128 = osb("ones128", [128, 1], BF16)
        onesrow = osb("onesrow", [1, 128], BF16)
        gate_sb = osb("gate_sb", [128, 128], BF16)
        bias_row = osb("bias_row", [1, 16], BF16)
        bias_tok = osb("bias_tok", [128, 16], F32)
        m_row = osb("m_row", [1, 512], F32)
        v_row = osb("v_row", [1, 512], F32)
        t_row = osb("t_row", [1, 512], F32)
        r_bf = osb("r_bf", [1, 512], BF16)
        mr_bf = osb("mr_bf", [1, 512], BF16)
        Rsb = osb("Rsb", [128, 512], BF16)
        MRsb = osb("MRsb", [128, 512], BF16)

        P = [octx.enter_context(nc.psum_tensor(f"P{i}", [128, 512], F32)) for i in range(8)]

        CONSTS = [1e-5, -0.5, 1e-30, 0.5, -0.125, -0.0625, 0.125, -1.0, 2.5, 1e-20]
        cpool = osb("cpool", [128, len(CONSTS)], F32)
        for i, cv in enumerate(CONSTS):
            nc.const_aps.register(F32, cv, cpool[:, i:i + 1])

        p.op("gpsimd", "iota", iota1[:], [[1, 128]], channel_multiplier=-1,
             allow_small_or_imprecise_dtypes=True)
        p.op("gpsimd", "iota", iota2[:], [[-128, 8], [64, 16]], channel_multiplier=-1,
             allow_small_or_imprecise_dtypes=True)
        p.op("gpsimd", "iota", E_sb[:], [[1, 16], [0, 128]], channel_multiplier=-1,
             allow_small_or_imprecise_dtypes=True)
        p.op("gpsimd", "memset", ones128[:], 1.0)
        for i, cv in enumerate(CONSTS):
            p.op("gpsimd", "memset", cpool[:, i:i + 1], cv)
        p.op("gpsimd", "memset", onesrow[:], 1.0, incs=[(sG, 1)])
        g_consts = sG.n
        p.op("gpsimd", "dma_start", gate_sb[:].rearrange("p (c e) -> p c e", e=16),
             gate_h.rearrange("(c p) e -> p c e", p=128), incs=[(dInit, 16)])
        p.op("gpsimd", "dma_start", bias_row[:], bias_h[None, :], incs=[(dInit, 16)])

        p.wait("vector", sG, g_consts)
        p.op("vector", "tensor_single_scalar", tri01[:], iota1[:], 0.0, ALU.is_ge)
        p.op("vector", "tensor_single_scalar", identf32[:], iota1[:], 0.0, ALU.is_equal)
        p.op("vector", "tensor_single_scalar", iota1[:], iota2[:], -64.0, ALU.is_gt)
        p.op("vector", "tensor_single_scalar", iota2[:], iota2[:], 0.0, ALU.is_le)
        p.op("vector", "tensor_single_scalar", E_sb[:], E_sb[:], 0.0, ALU.is_equal)
        p.op("vector", "tensor_mul", hselt[:], iota1[:], iota2[:], incs=[(sV, 1)])
        v_consts = sV.n

        p.wait("tensor", dInit, 64)
        p.wait("tensor", sV, v_consts)
        p.op("tensor", "matmul", P[4][:, 0:16], onesrow[:], bias_row[:],
             start=True, stop=True, incs=[(sT, 1)])
        p.wait("scalar", sT, sT.n)
        p.op("scalar", "copy", bias_tok[:], P[4][:, 0:16], incs=[(sS, 1)])
        s_biastok = sS.n

        sq2_free = [0, 0]
        ln_evac_s = 0
        ln_rows_v = 0

        with ExitStack() as ctx1:
            def sb1(name, shape, dt):
                return ctx1.enter_context(nc.sbuf_tensor(name, shape, dt))

            xin = [sb1(f"xin{i}", [128, NCH * 512], BF16) for i in range(2)]
            xn = sb1("xn", [128, NCH * 512], BF16)
            sq2 = [sb1(f"sq2_{i}", [128, 512], BF16) for i in range(2)]
            qf = sb1("qf", [128, NCH * 512], BF16)
            kroll = [sb1(f"kroll{i}", [128, NCH * 512], BF16) for i in range(2)]
            vroll = [sb1(f"vroll{i}", [128, 4 * 1024], BF16) for i in range(2)]
            probs = [sb1(f"probs{i}", [128, 512], BF16) for i in range(2)]
            aof = sb1("aof", [128, NCH * 512], BF16)
            sums_row = sb1("sums_row", [1, 8192], BF16)
            rrow = sb1("rrow", [1, 8192], BF16)
            spread = sb1("spread", [128, 64], BF16)
            rbsb = [sb1(f"rbsb{i}", [128, 512], BF16) for i in range(2)]
            qmax_sb = sb1("qmax_sb", [16, 1], F32)
            kmax_sb = sb1("kmax_sb", [16, 2], F32)
            u_sb = sb1("u_sb", [16, 1], F32)
            u2_sb = sb1("u2_sb", [16, 1], F32)
            B_sb = sb1("B_sb", [16, 1], F32)
            brow_bf = sb1("brow_bf", [1, 16], BF16)
            bias_all = sb1("bias_all", [128, 16], F32)
            wqkv_sb = sb1("wqkv_sb", [128, NCH * 3 * D], BF16)
            wout_sb = sb1("wout_sb", [128, NCH * D], BF16)

            p.op("gpsimd", "dma_start", wqkv_sb[:].rearrange("p (c n) -> p c n", n=3 * D),
                 wqkv_h.rearrange("(c p) n -> p c n", p=128), incs=[(dInit, 16)])
            p.op("gpsimd", "dma_start", wout_sb[:].rearrange("p (c n) -> p c n", n=D),
                 wout_h.rearrange("(c p) n -> p c n", p=128), incs=[(dInit, 16)])
            p.op("gpsimd", "dma_start", xin[0][:].rearrange("p (c t) -> p c t", t=512),
                 x_h.rearrange("(c p) t -> p c t", p=128)[:, :, 0:512], incs=[(dXin[0], 16)])
            nxin = [1, 0]

            wq = lambda fc, n: 3072 * fc + n

            pr_evac, pr_sem, pr_i = {}, {}, 0
            exp_done, pa_done, tri_v, ridx = {}, {}, {}, 0
            rb_evac, rb_mult, rb_i = {}, {}, 0
            rb_last_s = 0
            sums_evac_val = 0
            qkv_done_prev = 0
            xadd_done_prev = 0
            xmid_stores = 0

            p.wait("tensor", dInit, 64)
            p.wait("tensor", sS, s_biastok)

            for g in range(NCHUNK):
                c = g % 4
                par = g % 2

                if g + 1 < NCHUNK:
                    np_ = (g + 1) % 2
                    p.wait("gpsimd", sV, xadd_done_prev)
                    p.op("gpsimd", "dma_start",
                         xin[np_][:].rearrange("p (c t) -> p c t", t=512),
                         x_h.rearrange("(c p) t -> p c t", p=128)[:, :, 512 * (g + 1):512 * (g + 2)],
                         incs=[(dXin[np_], 16)])
                    nxin[np_] += 1

                p.wait("vector", dXin[par], 16 * nxin[par])
                p.wait("tensor", dXin[par], 16 * nxin[par])
                if g > 0:
                    p.wait("tensor", sS, sums_evac_val)
                    p.wait("tensor", sS, rb_last_s)
                for fc in range(NCH):
                    p.op("tensor", "matmul", P[3][0:1, :], ones128[:],
                         xin[par][:, 512 * fc:512 * (fc + 1)],
                         start=(fc == 0), stop=(fc == NCH - 1))
                for fc in range(NCH):
                    p.wait("vector", sT, sq2_free[fc % 2])
                    p.op("vector", "tensor_mul", sq2[fc % 2][:],
                         xin[par][:, 512 * fc:512 * (fc + 1)],
                         xin[par][:, 512 * fc:512 * (fc + 1)], incs=[(sV, 1)])
                    p.wait("tensor", sV, sV.n)
                    p.op("tensor", "matmul", P[4][0:1, :], ones128[:], sq2[fc % 2][:],
                         start=(fc == 0), stop=(fc == NCH - 1), incs=[(sT, 1)])
                    sq2_free[fc % 2] = sT.n
                p.wait("vector", sT, sT.n)
                p.op("vector", "tensor_single_scalar", m_row[:], P[3][0:1, :], 1.0 / D, ALU.mult)
                p.op("vector", "tensor_single_scalar", v_row[:], P[4][0:1, :], 1.0 / D, ALU.mult)
                p.op("vector", "tensor_mul", t_row[:], m_row[:], m_row[:])
                p.op("vector", "tensor_sub", v_row[:], v_row[:], t_row[:], incs=[(sV, 1)])
                ln_rows_v = sV.n
                p.wait("scalar", sV, sV.n)
                p.op("scalar", "activation", t_row[:], v_row[:], AF.Ln, bias=1e-5)
                p.op("scalar", "activation", r_bf[:], t_row[:], AF.Exp, scale=-0.5, incs=[(sS, 1)])
                p.wait("vector", sS, sS.n)
                p.op("vector", "tensor_mul", mr_bf[:], m_row[:], r_bf[:], incs=[(sV, 1)])
                p.wait("tensor", sV, sV.n)
                p.op("tensor", "matmul", P[4][:], onesrow[:], r_bf[:], start=True, stop=True)
                p.op("tensor", "matmul", P[5][:], onesrow[:], mr_bf[:], start=True, stop=True,
                     incs=[(sT, 1)])
                p.wait("scalar", sT, sT.n)
                p.op("scalar", "copy", Rsb[:], P[4][:])
                p.op("scalar", "copy", MRsb[:], P[5][:], incs=[(sS, 1)])
                ln_evac_s = sS.n
                p.wait("vector", sS, ln_evac_s)
                p.wait("vector", dXmW, 16 * xmid_stores)
                p.wait("vector", sT, qkv_done_prev)
                xn_v = {}
                for fc in range(NCH):
                    p.wait("vector", sT, sq2_free[fc % 2])
                    p.op("vector", "tensor_mul", sq2[fc % 2][:],
                         xin[par][:, 512 * fc:512 * (fc + 1)], Rsb[:])
                    p.op("vector", "tensor_sub", xn[:, 512 * fc:512 * (fc + 1)],
                         sq2[fc % 2][:], MRsb[:], incs=[(sV, 1)])
                    xn_v[fc] = sV.n
                    sq2_free[fc % 2] = 0

                def mmgroup(dst_ap, oper_fn):
                    nonlocal pr_i
                    bank = 6 + (pr_i % 2)
                    if pr_i >= 2:
                        p.wait("tensor", pr_sem[pr_i - 2], pr_evac[pr_i - 2])
                    for fc in range(NCH):
                        p.wait("tensor", sV, xn_v[fc])
                        lhs, rhs = oper_fn(fc)
                        p.op("tensor", "matmul", P[bank][:], lhs, rhs,
                             start=(fc == 0), stop=(fc == NCH - 1),
                             incs=([(sT, 1)] if fc == NCH - 1 else []))
                    p.wait("scalar", sT, sT.n)
                    p.op("scalar", "copy", dst_ap, P[bank][:], incs=[(sS, 1)])
                    pr_evac[pr_i], pr_sem[pr_i] = sS.n, sS
                    pr_i += 1

                for m in range(NCH):
                    mmgroup(kroll[par][:, 512 * m:512 * (m + 1)],
                            lambda fc, m=m: (wqkv_sb[:, wq(fc, D + 128 * m):wq(fc, D + 128 * m) + 128],
                                             xn[:, 512 * fc:512 * (fc + 1)]))
                for tt in range(4):
                    for half in range(2):
                        mmgroup(vroll[par][:, 1024 * tt + 512 * half:1024 * tt + 512 * (half + 1)],
                                lambda fc, tt=tt, half=half: (
                                    xn[:, 512 * fc + 128 * tt:512 * fc + 128 * (tt + 1)],
                                    wqkv_sb[:, wq(fc, 2 * D + 512 * half):wq(fc, 2 * D + 512 * (half + 1))]))
                for m in range(NCH):
                    mmgroup(qf[:, 512 * m:512 * (m + 1)],
                            lambda fc, m=m: (wqkv_sb[:, wq(fc, 128 * m):wq(fc, 128 * m) + 128],
                                             xn[:, 512 * fc:512 * (fc + 1)]))
                qkv_done_prev = sT.n

                p.wait("tensor", sS, pr_evac[pr_i - 1])
                p.wait("vector", sS, pr_evac[pr_i - 1])
                for which, bank in (("q", 3), ("k", 5)):
                    src = qf if which == "q" else kroll[par]
                    for fc in range(NCH):
                        p.wait("vector", sT, sq2_free[fc % 2])
                        p.op("vector", "tensor_mul", sq2[fc % 2][:],
                             src[:, 512 * fc:512 * (fc + 1)], src[:, 512 * fc:512 * (fc + 1)],
                             incs=[(sV, 1)])
                        p.wait("tensor", sV, sV.n)
                        p.op("tensor", "matmul", P[bank][0:16, :],
                             hselt[:, 16 * fc:16 * (fc + 1)], sq2[fc % 2][:],
                             start=(fc == 0), stop=(fc == NCH - 1), incs=[(sT, 1)])
                        sq2_free[fc % 2] = sT.n
                p.wait("vector", sT, sT.n)
                p.op("vector", "reduce_max", qmax_sb[:], P[3][0:16, :], AX.X)
                if c > 0:
                    p.op("vector", "reduce_max", kmax_sb[:, par:par + 1], P[5][0:16, :], AX.X)
                    p.op("vector", "tensor_max", u_sb[:], kmax_sb[:, 0:1], kmax_sb[:, 1:2],
                         incs=[(sV, 1)])
                    p.wait("scalar", sV, sV.n)
                    p.op("scalar", "add", u2_sb[:], u_sb[:], qmax_sb[:], incs=[(sS, 1)])
                else:
                    p.op("vector", "reduce_max", kmax_sb[:, par:par + 1], P[5][0:16, :], AX.X,
                         incs=[(sV, 1)])
                    p.wait("scalar", sV, sV.n)
                    p.op("scalar", "add", u2_sb[:], kmax_sb[:, par:par + 1], qmax_sb[:],
                         incs=[(sS, 1)])
                p.wait("tensor", sS, sS.n)
                p.op("tensor", "transpose", P[4][0:1, 0:16], u2_sb[:], identf32[0:16, 0:16],
                     incs=[(sT, 1)])
                p.wait("scalar", sT, sT.n)
                p.op("scalar", "copy", brow_bf[:], P[4][0:1, 0:16], incs=[(sS, 1)])
                p.wait("tensor", sS, sS.n)
                p.op("tensor", "matmul", P[4][:, 0:16], onesrow[:], brow_bf[:],
                     start=True, stop=True, incs=[(sT, 1)])
                p.wait("scalar", sT, sT.n)
                p.op("scalar", "activation", bias_all[:], P[4][:, 0:16], AF.Copy,
                     scale=-0.0625, incs=[(sS, 1)])

                for h in range(NH):
                    fch, off = h // 2, 64 * (h % 2)
                    kts = ([("prev", kt) for kt in range(4)] if c > 0 else []) \
                        + [("cur", kt) for kt in range(4)]
                    nkts = len(kts)

                    def emit_sumsav(pend):
                        rb_, qlo_, vsrc_, kt_, idx_, h_, rx = pend
                        if idx_ == nkts - 1:
                            p.wait("tensor", sV, tri_v[rx])
                        p.wait("tensor", sS, exp_done[rx])
                        p.op("tensor", "matmul", P[3][0:1, qlo_:512], ones128[:],
                             probs[rb_][:, qlo_:512], start=(idx_ == 0), stop=(idx_ == nkts - 1))
                        p.op("tensor", "matmul", P[2][off:off + 64, qlo_:512],
                             vsrc_[:, 1024 * kt_ + 64 * h_:1024 * kt_ + 64 * (h_ + 1)],
                             probs[rb_][:, qlo_:512], start=(idx_ == 0), stop=(idx_ == nkts - 1),
                             incs=[(sT, 1)])
                        pa_done[rx] = sT.n

                    pending = None
                    for idx, (which, kt) in enumerate(kts):
                        rb = ridx % 2
                        qlo = 0 if which == "prev" else 128 * kt
                        ksrc = kroll[(g - 1) % 2] if which == "prev" else kroll[par]
                        vsrc = vroll[(g - 1) % 2] if which == "prev" else vroll[par]
                        if ridx >= 2:
                            p.wait("tensor", sS, exp_done[ridx - 2])
                            p.wait("tensor", sV, tri_v[ridx - 2])
                        p.op("tensor", "matmul", P[rb][:, qlo:512],
                             ksrc[off:off + 64, 512 * fch + 128 * kt:512 * fch + 128 * (kt + 1)],
                             qf[off:off + 64, 512 * fch + qlo:512 * (fch + 1)],
                             start=True, stop=True, incs=[(sT, 1)])
                        t_sc = sT.n
                        if pending is not None:
                            emit_sumsav(pending)
                        p.wait("scalar", sT, t_sc)
                        if ridx >= 2:
                            p.wait("scalar", sT, pa_done[ridx - 2])
                        p.op("scalar", "activation", probs[rb][:, qlo:512], P[rb][:, qlo:512],
                             AF.Exp, bias=bias_all[:, h:h + 1], scale=0.125, incs=[(sS, 1)])
                        exp_done[ridx] = sS.n
                        if which == "cur":
                            p.wait("vector", sS, exp_done[ridx])
                            p.op("vector", "tensor_mul", probs[rb][:, qlo:qlo + 128],
                                 probs[rb][:, qlo:qlo + 128], tri01[:], incs=[(sV, 1)])
                            tri_v[ridx] = sV.n
                        else:
                            tri_v[ridx] = 0
                        pending = (rb, qlo, vsrc, kt, idx, h, ridx)
                        ridx += 1
                    emit_sumsav(pending)
                    p.wait("scalar", sT, sT.n)
                    p.op("scalar", "copy", sums_row[0:1, 512 * h:512 * (h + 1)], P[3][0:1, :])
                    p.op("scalar", "copy", aof[off:off + 64, 512 * fch:512 * (fch + 1)],
                         P[2][off:off + 64, :], incs=[(sS, 1)])
                    sums_evac_val = sS.n
                    p.wait("tensor", sS, sums_evac_val)

                p.wait("sync", sS, sums_evac_val)
                nA = 32 * (g // 2)
                p.op("sync", "dma_start", srow_h[par][None, :], sums_row[:],
                     incs=[(dSrA[par], 16)])
                p.wait("sync", dSrA[par], nA + 16)
                p.op("sync", "dma_start", spread[:],
                     srow_h[par].rearrange("(p j) -> p j", p=128), incs=[(dSrA[par], 16)])
                p.wait("vector", dSrA[par], nA + 32)
                p.op("vector", "reciprocal", spread[:], spread[:], incs=[(sV, 1)])
                p.wait("sync", sV, sV.n)
                nB = 32 * (g // 2)
                p.op("sync", "dma_start", rrow_h[par].rearrange("(p j) -> p j", p=128),
                     spread[:], incs=[(dSrB[par], 16)])
                p.wait("sync", dSrB[par], nB + 16)
                p.op("sync", "dma_start", rrow[:], rrow_h[par][None, :], incs=[(dSrB[par], 16)])
                p.wait("tensor", dSrB[par], nB + 32)
                for h in range(NH):
                    fch, off = h // 2, 64 * (h % 2)
                    bank = 4 + (rb_i % 2)
                    if rb_i >= 2:
                        p.wait("tensor", sS, rb_evac[rb_i - 2])
                    p.op("tensor", "matmul", P[bank][off:off + 64, :], onesrow[0:1, 0:64],
                         rrow[0:1, 512 * h:512 * (h + 1)], start=True, stop=True, incs=[(sT, 1)])
                    p.wait("scalar", sT, sT.n)
                    if rb_i >= 2:
                        p.wait("scalar", sV, rb_mult[rb_i - 2])
                    p.op("scalar", "copy", rbsb[rb_i % 2][off:off + 64, :],
                         P[bank][off:off + 64, :], incs=[(sS, 1)])
                    rb_evac[rb_i] = sS.n
                    p.wait("vector", sS, rb_evac[rb_i])
                    p.op("vector", "tensor_mul", aof[off:off + 64, 512 * fch:512 * (fch + 1)],
                         aof[off:off + 64, 512 * fch:512 * (fch + 1)],
                         rbsb[rb_i % 2][off:off + 64, :], incs=[(sV, 1)])
                    rb_mult[rb_i] = sV.n
                    rb_i += 1
                rb_last_s = rb_evac[rb_i - 1]
                v_aof = sV.n

                p.wait("tensor", sV, v_aof)
                for m in range(NCH):
                    bank = 6 + (pr_i % 2)
                    if pr_i >= 2:
                        p.wait("tensor", pr_sem[pr_i - 2], pr_evac[pr_i - 2])
                    for fc in range(NCH):
                        p.op("tensor", "matmul", P[bank][:],
                             wout_sb[:, 1024 * fc + 128 * m:1024 * fc + 128 * (m + 1)],
                             aof[:, 512 * fc:512 * (fc + 1)],
                             start=(fc == 0), stop=(fc == NCH - 1),
                             incs=([(sT, 1)] if fc == NCH - 1 else []))
                    p.wait("vector", sT, sT.n)
                    p.op("vector", "tensor_add", xn[:, 512 * m:512 * (m + 1)], P[bank][:],
                         xin[par][:, 512 * m:512 * (m + 1)], incs=[(sV, 1)])
                    pr_evac[pr_i], pr_sem[pr_i] = sV.n, sV
                    pr_i += 1
                xadd_done_prev = sV.n
                p.wait("sync", sV, xadd_done_prev)
                p.op("sync", "dma_start",
                     xmid_h.rearrange("(c p) t -> p c t", p=128)[:, :, 512 * g:512 * (g + 1)],
                     xn[:].rearrange("p (c t) -> p c t", t=512), incs=[(dXmW, 16)])
                xmid_stores += 1

            p.wait("sync", dXmW, 16 * xmid_stores)
            p.wait("gpsimd", dXin[0], 16 * nxin[0])
            p.wait("gpsimd", dXin[1], 16 * nxin[1])
            p.op("vector", "tensor_copy", t_row[:], m_row[:], incs=[(sPh, 1)])
            p.op("scalar", "copy", v_row[:], m_row[:], incs=[(sPh, 1)])
            for e in ("sync", "gpsimd", "tensor"):
                p.op(e, "nop", incs=[(sPh, 1)])
            for e in ENGINES:
                p.wait(e, sPh, 5)

        with ExitStack() as ctx2:
            def sb2(name, shape, dt):
                return ctx2.enter_context(nc.sbuf_tensor(name, shape, dt))

            xres = sb2("xres", [128, NCH * TPC], BF16)
            xn2 = sb2("xn2", [128, NCH * TPC], BF16)
            wb = [sb2(f"wb{i}", [128, 4096], BF16) for i in range(3)]
            H = [sb2(f"H{i}", [128, 512], BF16) for i in range(4)]
            h1 = [sb2(f"h1_{i}", [128, 512], BF16) for i in range(2)]
            cwb = [sb2(f"cwb{i}", [128, 512], BF16) for i in range(2)]
            sq2b = h1
            scorestok = sb2("scorestok", [128, 512], F32)
            sctok = sb2("sctok", [128, 512], F32)
            m32 = sb2("m32", [128, 32], F32)
            eqb = sb2("eqb", [128, 512], BF16)
            pfx = [sb2(f"pfx{i}", [128, 512], BF16) for i in range(2)]
            selb = sb2("selb", [128, 512], BF16)
            selsum = sb2("selsum", [128, 512], BF16)
            wtok = sb2("wtok", [128, 512], F32)
            exl = wtok
            wsum = sb2("wsum", [128, 32], F32)
            wsum2 = sb2("wsum2", [128, 32], F32)
            winv = sb2("winv", [128, 32], F32)
            gp6 = sb2("gp6", [128, 768], F32)
            gtree = sb2("gtree", [128, 512], F32)
            gcmp = gp6
            granks = gtree
            gsc = sb2("gsc", [128, 128], F32)
            gmask = sb2("gmask", [128, 128], F32)
            cmb_f = sb2("cmb_f", [16, 4096], BF16)
            stage = [sb2(f"stage{i}", [128, 512], F32) for i in range(2)]

            p.op("sync", "dma_start", xres[:].rearrange("p (c t) -> p c t", t=TPC),
                 xmid_h.rearrange("(c p) t -> p c t", p=128), incs=[(dXmR, 16)])
            p.wait("vector", dXmR, 16)
            p.wait("tensor", dXmR, 16)

            def load_w(slot, e, is_up):
                src = upT_h[e] if is_up else downT_h[e]
                inner = 512 if is_up else D
                p.op("gpsimd", "dma_start",
                     wb[slot][:].rearrange("p (c i) -> p c i", i=inner),
                     src.rearrange("(c p) i -> p c i", p=128), incs=[(dWgt[slot], 16)])

            slot_loads = [0, 0, 0]
            su = [0] * NEXP
            sd = [0] * NEXP
            su[0], sd[0], su[1] = 0, 1, 2
            for e in range(1, NEXP):
                if e >= 2:
                    su[e] = sd[e - 2]
                sd[e] = su[e - 1]
            load_w(0, 0, True); slot_loads[0] += 1
            load_w(1, 0, False); slot_loads[1] += 1
            load_w(2, 1, True); slot_loads[2] += 1

            logit_mm_done = 0
            sq2_free = [0, 0]
            for tau in range(NTAU):
                if tau > 0:
                    p.wait("tensor", sS, ln_evac_s)
                    p.wait("tensor", sV, ln_rows_v)
                for fc in range(NCH):
                    p.op("tensor", "matmul", P[3][0:1, :], ones128[:],
                         xres[:, TPC * fc + 512 * tau:TPC * fc + 512 * (tau + 1)],
                         start=(fc == 0), stop=(fc == NCH - 1))
                for fc in range(NCH):
                    p.wait("vector", sT, sq2_free[fc % 2])
                    p.op("vector", "tensor_mul", sq2b[fc % 2][:],
                         xres[:, TPC * fc + 512 * tau:TPC * fc + 512 * (tau + 1)],
                         xres[:, TPC * fc + 512 * tau:TPC * fc + 512 * (tau + 1)], incs=[(sV, 1)])
                    p.wait("tensor", sV, sV.n)
                    p.op("tensor", "matmul", P[4][0:1, :], ones128[:], sq2b[fc % 2][:],
                         start=(fc == 0), stop=(fc == NCH - 1), incs=[(sT, 1)])
                    sq2_free[fc % 2] = sT.n
                p.wait("vector", sT, sT.n)
                p.op("vector", "tensor_single_scalar", m_row[:], P[3][0:1, :], 1.0 / D, ALU.mult)
                p.op("vector", "tensor_single_scalar", v_row[:], P[4][0:1, :], 1.0 / D, ALU.mult)
                p.op("vector", "tensor_mul", t_row[:], m_row[:], m_row[:])
                p.op("vector", "tensor_sub", v_row[:], v_row[:], t_row[:], incs=[(sV, 1)])
                ln_rows_v = sV.n
                p.wait("scalar", sV, sV.n)
                p.op("scalar", "activation", t_row[:], v_row[:], AF.Ln, bias=1e-5)
                p.op("scalar", "activation", r_bf[:], t_row[:], AF.Exp, scale=-0.5, incs=[(sS, 1)])
                p.wait("vector", sS, sS.n)
                p.op("vector", "tensor_mul", mr_bf[:], m_row[:], r_bf[:], incs=[(sV, 1)])
                p.wait("tensor", sV, sV.n)
                p.op("tensor", "matmul", P[4][:], onesrow[:], r_bf[:], start=True, stop=True)
                p.op("tensor", "matmul", P[5][:], onesrow[:], mr_bf[:], start=True, stop=True,
                     incs=[(sT, 1)])
                p.wait("scalar", sT, sT.n)
                p.op("scalar", "copy", Rsb[:], P[4][:])
                p.op("scalar", "copy", MRsb[:], P[5][:], incs=[(sS, 1)])
                ln_evac_s = sS.n
                p.wait("vector", sS, ln_evac_s)
                for fc in range(NCH):
                    p.wait("vector", sT, sq2_free[fc % 2])
                    p.op("vector", "tensor_mul", sq2b[fc % 2][:],
                         xres[:, TPC * fc + 512 * tau:TPC * fc + 512 * (tau + 1)], Rsb[:])
                    p.op("vector", "tensor_sub",
                         xn2[:, TPC * fc + 512 * tau:TPC * fc + 512 * (tau + 1)],
                         sq2b[fc % 2][:], MRsb[:], incs=[(sV, 1)])
                    sq2_free[fc % 2] = 0
                xn2_tau_v = sV.n
                p.wait("tensor", sV, xn2_tau_v)
                for tt in range(4):
                    for fc in range(NCH):
                        p.op("tensor", "matmul",
                             P[6][:, 64 * tau + 16 * tt:64 * tau + 16 * (tt + 1)],
                             xn2[:, TPC * fc + 512 * tau + 128 * tt:TPC * fc + 512 * tau + 128 * (tt + 1)],
                             gate_sb[:, 16 * fc:16 * (fc + 1)],
                             start=(fc == 0), stop=(fc == NCH - 1),
                             incs=([(sT, 1)] if fc == NCH - 1 else []))
                logit_mm_done = sT.n

            p.wait("scalar", sT, logit_mm_done)
            for j in range(4):
                p.op("scalar", "activation", exl[:, 128 * j:128 * (j + 1)],
                     P[6][:, 128 * j:128 * (j + 1)], AF.Exp, scale=-1.0,
                     incs=([(sS, 1)] if j == 3 else []))
            s_exl = sS.n
            p.wait("vector", sS, s_exl)
            p.op("vector", "tensor_single_scalar", exl[:], exl[:], 1.0, ALU.add)
            p.op("vector", "reciprocal", scorestok[:], exl[:])
            sct3 = sctok[:].rearrange("p (t e) -> p t e", e=16)
            sco3 = scorestok[:].rearrange("p (t e) -> p t e", e=16)
            p.op("vector", "tensor_tensor", sct3, sco3,
                 bias_tok[:, None, :].broadcast_to([128, 32, 16]), ALU.add)

            scg = sctok[:].rearrange("p (gg e) -> p gg e", e=4)
            pairs = [(0, 1), (0, 2), (0, 3), (1, 2), (1, 3), (2, 3)]
            for i, (a, b) in enumerate(pairs):
                p.op("vector", "tensor_tensor", gp6[:, 128 * i:128 * (i + 1), None],
                     scg[:, :, a:a + 1], scg[:, :, b:b + 1], ALU.add)
            p.op("vector", "tensor_max", gtree[:, 0:128], gp6[:, 0:128], gp6[:, 128:256])
            p.op("vector", "tensor_max", gtree[:, 128:256], gp6[:, 256:384], gp6[:, 384:512])
            p.op("vector", "tensor_max", gtree[:, 256:384], gp6[:, 512:640], gp6[:, 640:768])
            p.op("vector", "tensor_max", gtree[:, 384:512], gtree[:, 0:128], gtree[:, 128:256])
            p.op("vector", "tensor_copy", pfx[0][:], scorestok[:])
            p.op("vector", "tensor_max", gsc[:], gtree[:, 384:512], gtree[:, 256:384])
            p.op("vector", "tensor_copy", pfx[1][:], scorestok[:])
            g3 = gsc[:].rearrange("p (t g) -> p t g", g=4)
            gm3 = gmask[:].rearrange("p (t g) -> p t g", g=4)
            ci = 0
            for j in range(4):
                for mth in range(4):
                    if mth == j:
                        continue
                    opc = ALU.is_ge if mth < j else ALU.is_gt
                    p.op("vector", "tensor_tensor", gcmp[:, 32 * ci:32 * (ci + 1), None],
                         g3[:, :, mth:mth + 1], g3[:, :, j:j + 1], opc)
                    ci += 1
            for j in range(4):
                p.op("vector", "tensor_add", granks[:, 32 * j:32 * (j + 1)],
                     gcmp[:, 96 * j:96 * j + 32], gcmp[:, 96 * j + 32:96 * j + 64])
            for j in range(4):
                p.op("vector", "tensor_add", granks[:, 32 * j:32 * (j + 1)],
                     granks[:, 32 * j:32 * (j + 1)], gcmp[:, 96 * j + 64:96 * j + 96])
            for j in range(4):
                p.op("vector", "tensor_single_scalar", gm3[:, :, j:j + 1],
                     granks[:, 32 * j:32 * (j + 1), None], 1.5, ALU.is_le)
            p.op("vector", "tensor_add", gtree[:, 128:256], gsc[:], gsc[:])
            p.op("vector", "tensor_add", gtree[:, 128:256], gsc[:], gsc[:])
            p.op("vector", "tensor_tensor", scg, scg,
                 gmask[:, :, None].broadcast_to([128, 128, 4]), ALU.mult)

            eq3 = eqb[:].rearrange("p (t e) -> p t e", e=16)
            for r in range(4):
                p.op("vector", "reduce_max", m32[:], sct3, AX.X)
                p.op("vector", "tensor_tensor", eq3, sct3,
                     m32[:, :, None].broadcast_to([128, 32, 16]), ALU.is_ge)
                cur = eqb
                for si, s in enumerate((1, 2, 4, 8)):
                    nxt = pfx[si % 2]
                    cv = cur[:].rearrange("p (t e) -> p t e", e=16)
                    nv = nxt[:].rearrange("p (t e) -> p t e", e=16)
                    p.op("vector", "tensor_copy", nv[:, :, 0:s], cv[:, :, 0:s])
                    p.op("vector", "tensor_tensor", nv[:, :, s:16], cv[:, :, s:16],
                         cv[:, :, 0:16 - s], ALU.add)
                    cur = nxt
                p.op("vector", "tensor_single_scalar", gp6[:, 0:512], cur[:], 1.0,
                     ALU.is_equal)
                p.op("vector", "tensor_mul", selb[:], eqb[:], gp6[:, 0:512])
                if r == 0:
                    p.op("vector", "tensor_copy", selsum[:], selb[:])
                else:
                    p.op("vector", "tensor_add", selsum[:], selsum[:], selb[:])
                if r < 3:
                    p.op("vector", "scalar_tensor_tensor", sctok[:], selb[:], -1e30,
                         sctok[:], ALU.mult, ALU.add)
            p.op("vector", "tensor_mul", wtok[:], scorestok[:], selsum[:])
            w3 = wtok[:].rearrange("p (t e) -> p t e", e=16)
            p.op("vector", "reduce_sum", wsum[:], w3, AX.X)
            p.op("vector", "tensor_scalar_add", wsum2[:], wsum[:],
                 cpool[:, CONSTS.index(1e-20):CONSTS.index(1e-20) + 1])
            p.op("vector", "tensor_add", gp6[:, 0:512], scorestok[:], scorestok[:])
            p.op("vector", "tensor_add", gp6[:, 0:512], scorestok[:], scorestok[:])
            p.op("vector", "reciprocal", winv[:], wsum2[:])
            p.op("vector", "tensor_add", gp6[:, 0:512], scorestok[:], scorestok[:])
            p.op("vector", "tensor_tensor", w3, w3,
                 winv[:, :, None].broadcast_to([128, 32, 16]), ALU.mult, incs=[(sV, 1)])
            v_comb = sV.n

            p.wait("tensor", sV, v_comb)
            p.wait("tensor", sS, ln_evac_s)
            cmb_evac = {}
            for tt in range(32):
                bank = 4 + (tt % 2)
                if tt >= 2:
                    p.wait("tensor", sS, cmb_evac[tt - 2])
                p.op("tensor", "transpose", P[bank][0:16, 0:128],
                     wtok[:, 16 * tt:16 * (tt + 1)], identf32[:], incs=[(sT, 1)])
                p.wait("scalar", sT, sT.n)
                p.op("scalar", "copy", cmb_f[:, 128 * tt:128 * (tt + 1)],
                     P[bank][0:16, 0:128], incs=[(sS, 1)])
                cmb_evac[tt] = sS.n
            s_cmb = sS.n

            H_free = 0
            pending_dn = None
            dn_wait = {}
            upp_free = [0, 0]
            dwp_free = [0, 0]
            cwb_free = [0, 0]
            cwb_s = [0, 0]
            h1_free = [0, 0]
            stage_dma = [0, 0]

            for e in range(NEXP):
                p.wait("tensor", dWgt[su[e]], 16 * slot_loads[su[e]])
                dn_wait[e] = 16 * slot_loads[sd[e]]
                def emit_down(pend):
                    nonlocal H_free
                    e_, tau_, hv = pend
                    p.wait("tensor", sV, hv)
                    p.wait("tensor", dWgt[sd[e_]], dn_wait[e_])
                    for n in range(NCH):
                        db = n % 2
                        p.wait("tensor", sV, dwp_free[db])
                        for kc in range(4):
                            p.op("tensor", "matmul", P[2 + db][:],
                                 wb[sd[e_]][:, 1024 * kc + 128 * n:1024 * kc + 128 * (n + 1)],
                                 H[kc][:], start=(kc == 0), stop=(kc == 3),
                                 incs=([(sT, 1)] if kc == 3 else []))
                        p.wait("vector", sT, sT.n)
                        xslice = xres[:, TPC * n + 512 * tau_:TPC * n + 512 * (tau_ + 1)]
                        if e_ < NEXP - 1:
                            p.op("vector", "tensor_add", xslice, P[2 + db][:], xslice,
                                 incs=[(sV, 1)])
                        else:
                            sp = n % 2
                            p.wait("vector", dOut[sp], 16 * stage_dma[sp])
                            p.op("vector", "tensor_add", stage[sp][:], P[2 + db][:], xslice,
                                 incs=[(sV, 1)])
                            p.wait("sync", sV, sV.n)
                            p.op("sync", "dma_start",
                                 y_h[128 * n:128 * (n + 1), 512 * tau_:512 * (tau_ + 1)],
                                 stage[sp][:], incs=[(dOut[sp], 16)])
                            stage_dma[sp] += 1
                        dwp_free[db] = sV.n
                    H_free = sT.n

                for tau in range(NTAU):
                    if e < 16:
                        cb = tau % 2
                        p.wait("tensor", sV, cwb_free[cb])
                        p.wait("tensor", sS, s_cmb)
                        p.op("tensor", "matmul", P[6 + cb][:], E_sb[:, 128 * e:128 * (e + 1)],
                             cmb_f[:, 512 * tau:512 * (tau + 1)], start=True, stop=True,
                             incs=[(sT, 1)])
                        p.wait("scalar", sT, sT.n)
                        p.op("scalar", "activation", cwb[cb][:], P[6 + cb][:], AF.Copy,
                             scale=2.5, incs=[(sS, 1)])
                        cwb_s[cb] = sS.n
                    h_last_v = 0
                    for m in range(4):
                        ub = m % 2
                        p.wait("tensor", sS, upp_free[ub])
                        for kc in range(NCH):
                            p.op("tensor", "matmul", P[ub][:],
                                 wb[su[e]][:, 512 * kc + 128 * m:512 * kc + 128 * (m + 1)],
                                 xn2[:, TPC * kc + 512 * tau:TPC * kc + 512 * (tau + 1)],
                                 start=(kc == 0), stop=(kc == NCH - 1),
                                 incs=([(sT, 1)] if kc == NCH - 1 else []))
                        t_up = sT.n
                        if m == 0 and pending_dn is not None:
                            emit_down(pending_dn)
                            pending_dn = None
                        p.wait("scalar", sT, t_up)
                        p.wait("scalar", sV, h1_free[ub])
                        p.op("scalar", "activation", h1[ub][:], P[ub][:], AF.Relu,
                             incs=[(sS, 1)])
                        upp_free[ub] = sS.n
                        p.wait("vector", sS, upp_free[ub])
                        p.wait("vector", sT, H_free)
                        if e < 16:
                            p.wait("vector", sS, cwb_s[tau % 2])
                            p.op("vector", "tensor_mul", H[m][:], h1[ub][:], cwb[tau % 2][:])
                            p.op("vector", "tensor_mul", H[m][:], H[m][:], h1[ub][:],
                                 incs=[(sV, 1)])
                        else:
                            p.op("vector", "tensor_mul", H[m][:], h1[ub][:], h1[ub][:],
                                 incs=[(sV, 1)])
                        h1_free[ub] = sV.n
                        h_last_v = sV.n
                    if e < 16:
                        cwb_free[tau % 2] = h_last_v
                    pending_dn = (e, tau, h_last_v)
                if pending_dn is not None:
                    emit_down(pending_dn)
                    pending_dn = None
                if e + 1 < NEXP:
                    p.wait("gpsimd", sT, H_free)
                    load_w(su[e], e + 1, False)
                    slot_loads[su[e]] += 1
                if e + 2 < NEXP:
                    p.wait("gpsimd", sT, H_free)
                    load_w(sd[e], e + 2, True)
                    slot_loads[sd[e]] += 1

            p.wait("sync", dOut[0], 16 * stage_dma[0])
            p.wait("sync", dOut[1], 16 * stage_dma[1])

        for _e in ("vector", "scalar"):
            p.add_engine_drains(_e)
        p.simulate()
        with nc.allow_low_precision(reason="bf16 residual stream / exact small-int masks; "
                                           "error budget analyzed vs rel-L2 4.8e-2 bar"):
            with nc.Block() as block:
                p.replay_into(block)
    return nc, p


_NC_CACHE = {}


def _get_nc():
    if "nc" not in _NC_CACHE:
        _NC_CACHE["nc"] = build_nc()
    return _NC_CACHE["nc"]


def kernel(L0_attn_x, L0_attn_w_qkv, L0_attn_w_out, L0_ffn_gate_w, L0_ffn_bias_e,
           L0_ffn_up_w, L0_ffn_down_w, L0_ffn_shared_up, L0_ffn_shared_down):
    x = np.asarray(L0_attn_x, dtype=np.float32)
    wqkv = np.ascontiguousarray(np.asarray(L0_attn_w_qkv, dtype=np.float32))
    wout = np.ascontiguousarray(np.asarray(L0_attn_w_out, dtype=np.float32))
    gate = np.ascontiguousarray(np.asarray(L0_ffn_gate_w, dtype=np.float32).T)
    biase = np.ascontiguousarray(np.asarray(L0_ffn_bias_e, dtype=np.float32))
    up_w = np.asarray(L0_ffn_up_w, dtype=np.float32)
    down_w = np.asarray(L0_ffn_down_w, dtype=np.float32)
    sup = np.asarray(L0_ffn_shared_up, dtype=np.float32)
    sdn = np.asarray(L0_ffn_shared_down, dtype=np.float32)

    upT = np.ascontiguousarray(np.concatenate(
        [up_w.transpose(0, 2, 1), sup.reshape(4, 512, D).transpose(0, 2, 1)], axis=0))
    downT = np.ascontiguousarray(np.concatenate(
        [down_w.transpose(0, 2, 1), np.ascontiguousarray(sdn.T).reshape(4, 512, D)], axis=0))

    nc, _ = _get_nc()
    in_maps = []
    for ci in range(NCORES):
        xc = np.ascontiguousarray(x[2 * ci:2 * ci + 2].reshape(TPC, D).T)
        in_maps.append({"x": xc, "wqkv": wqkv, "wout": wout, "gate": gate,
                        "biase": biase, "upT": upT, "downT": downT})
    res = run_bass_kernel_spmd(nc, in_maps, core_ids=list(range(NCORES)))
    y = np.empty((16, 2048, 1024), dtype=np.float32)
    for ci in range(NCORES):
        y[2 * ci:2 * ci + 2] = res.results[ci]["y"].T.reshape(2, 2048, D)
    return y


if __name__ == "__main__":
    nc, p = build_nc()
    print("instructions:", p.n_instr())
